# version 12
# baseline (speedup 1.0000x reference)
import numpy as np
import ml_dtypes
from contextlib import ExitStack

import concourse.bass as bass
import concourse.mybir as mybir
from concourse.bass_utils import run_bass_kernel_spmd

F32 = mybir.dt.float32
BF16 = mybir.dt.bfloat16
AF = mybir.ActivationFunctionType
OP = mybir.AluOpType
BF = ml_dtypes.bfloat16

B, S, D, H, HD, WIN = 2, 8192, 2048, 16, 128, 512
TOWN = 2048
CTX = TOWN + WIN
NI = D // 128
NT0 = CTX // 128
NQB = TOWN // WIN
DFF = 8192
SCALE = float(HD) ** -0.5
NEGM = -1.0e9
EPS = 1e-5

DEBUG_TAPS = ()
P1_SWEEPS = ("q", "ka", "kb", "va", "vb")
DMA_SCRATCH = 16384
NEXPB = 4
MAX_PHASE = 9


def build_nc():
    nc = bass.Bass("TRN2", target_bir_lowering=False,
                   dynamic_dma_scratch_size=DMA_SCRATCH)

    def dram_in(name, shape, dtype=F32):
        return nc.dram_tensor(name, shape, dtype, kind="ExternalInput").ap()

    def dram_tmp(name, shape, dtype):
        kind = "ExternalOutput" if name in DEBUG_TAPS else "Internal"
        return nc.dram_tensor(name, shape, dtype, kind=kind).ap()

    x_ctx = dram_in("x_ctx", [CTX, D])
    wq_t = dram_in("wq_t", [D, D])
    wk_t = dram_in("wk_t", [D, D])
    wv_t = dram_in("wv_t", [D, D])
    wo_t = dram_in("wo_t", [D, D])
    w0_t = dram_in("w0_t", [D, DFF])
    w1_t = dram_in("w1_t", [DFF, D])
    bias0_t = dram_in("bias0_t", [128, 64])
    bias1_r = dram_in("bias1_r", [1, D])
    cc_in = dram_in("cc", [128, CTX])
    ss_in = dram_in("ss", [128, CTX])
    msk_in = dram_in("msk", [128, 5 * 640], BF16)
    idm_in = dram_in("idm", [128, 128], BF16)

    q_sp = dram_tmp("q_sp", [D, TOWN], BF16)
    k_sp = dram_tmp("k_sp", [D, CTX], BF16)
    v_sp = dram_tmp("v_sp", [CTX, D], BF16)
    vT_sp = dram_tmp("vT_sp", [D, CTX], BF16)
    y_sp = dram_tmp("y_sp", [TOWN, D], F32)
    ln1t_sp = (dram_tmp("ln1t_sp", [128, NI * CTX], BF16)
               if "ln1t_sp" in DEBUG_TAPS else None)
    l2_sp = dram_tmp("l2_sp", [TOWN, D], BF16)
    g_sp = dram_tmp("g_sp", [128, 64 * TOWN], BF16)

    out = nc.dram_tensor("out", [TOWN, D], F32, kind="ExternalOutput").ap()

    ALL_SWEEPS = [("q", [0, 1, 2, 3], WIN, True),
                  ("ka", [0, 1, 2], 0, True),
                  ("kb", [3, 4], 0, True),
                  ("va", [0, 1, 2], 0, False),
                  ("vb", [3, 4], 0, False)]
    SWEEPS = [s for s in ALL_SWEEPS if s[0] in P1_SWEEPS]
    DO_V = any(nm in P1_SWEEPS for nm in ("va", "vb"))
    NP1_G = 16 * len(SWEEPS)
    P1_T = sum(16 * len(s[1]) for s in SWEEPS)
    NROPE_G = 16 * sum(1 for s in SWEEPS if s[3])

    _wmap = {"q": wq_t, "ka": wk_t, "kb": wk_t, "va": wv_t, "vb": wv_t}
    WLOADS = [(_wmap[nm], oh) for nm, _, _, _ in SWEEPS for oh in range(2)]
    W1P = [0, 0]
    for li in range(len(WLOADS)):
        W1P[li % 2] += 256
    W3P = [1024, 1024]

    ctx = ExitStack()
    with ctx:
        e = ctx.enter_context

        idm = e(nc.sbuf_tensor("idm_s", [128, 128], BF16))
        eps_t = e(nc.sbuf_tensor("eps_t", [128, 1], F32))
        ones1 = e(nc.sbuf_tensor("ones1", [1, 128], BF16))
        stats = e(nc.sbuf_tensor("stats", [128, 96], F32))
        aggr = e(nc.sbuf_tensor("aggr", [128, 8], F32))
        rstdl = e(nc.sbuf_tensor("rstdl", [128, 4], F32))
        rstd = e(nc.sbuf_tensor("rstd", [128, 4], F32))
        nmr = e(nc.sbuf_tensor("nmr", [128, 4], F32))

        semnames = [
            "sBar", "sStat", "sRstd", "sNmr", "sLn1", "sTp",
            "sLnT", "sPs", "sEv", "sRope", "sStg", "sPEh",
            "sRing", "sExp", "sExpN", "sTpE", "sExpT", "sPV", "sOutT",
            "sQbPE", "sWoPE", "sWoEv", "sYd", "sLn2d", "sGel", "sF1",
            "sOb", "sMisc", "sF2",
            "sXa", "sXb", "sWa", "sWb", "sWo2", "sXha", "sXhb",
            "sYspA", "sYspB", "sL2sp", "sL2a", "sL2b",
            "sSpQa", "sSpQb", "sSpVa", "sSpVb", "sGa", "sGb",
            "sGqa", "sGqb", "sYha", "sYhb", "sOua", "sOub", "sDv", "sAc", "sB1",
            "sStgV", "sVta", "sVtb", "sTpV", "sEvV", "sMsk", "sXc", "sXd",
            "sW4a", "sW4b", "sB1b", "sPl", "sRc", "sDg", "sXhF",
        ]
        sem = {n: e(nc.semaphore(n)) for n in semnames}

        NENG = 5
        bar_n = [0]

        def barrier(eng):
            eng.sem_inc(sem["sBar"], 1)
            eng.wait_ge(sem["sBar"], NENG * bar_n[0])

        dv = [0]

        class _VW:
            def __init__(self, v):
                self._v = v

            def wait_ge(self, *a, **k):
                return self._v.wait_ge(*a, **k)

            def sem_inc(self, *a, **k):
                return self._v.sem_inc(*a, **k)

            def sig(self, sh, k=1):
                self._v.wait_ge(sem["sDv"], dv[0])
                self._v.sem_inc(sh, k)

            def __getattr__(self, name):
                fn = getattr(self._v, name)

                def call(*a, **k):
                    self._v.wait_ge(sem["sDv"], dv[0])
                    ins = fn(*a, **k)
                    dv[0] += 1
                    ins.then_inc(sem["sDv"], 1)
                    return ins
                return call

        ac = [0]

        class _AW:
            def __init__(self, s):
                self._s = s

            def wait_ge(self, *a, **k):
                return self._s.wait_ge(*a, **k)

            def sem_inc(self, *a, **k):
                return self._s.sem_inc(*a, **k)

            def sig(self, sh, k=1):
                self._s.wait_ge(sem["sAc"], ac[0])
                self._s.sem_inc(sh, k)

            def __getattr__(self, name):
                fn = getattr(self._s, name)

                def call(*a, **k):
                    ins = fn(*a, **k)
                    ac[0] += 1
                    ins.then_inc(sem["sAc"], 1)
                    return ins
                return call

        pl = [0]

        class _PW:
            def __init__(self, g):
                self._g = g

            def wait_ge(self, *a, **k):
                return self._g.wait_ge(*a, **k)

            def sem_inc(self, *a, **k):
                return self._g.sem_inc(*a, **k)

            def sig(self, sh, k=1):
                self._g.wait_ge(sem["sPl"], pl[0])
                self._g.sem_inc(sh, k)

            def __getattr__(self, name):
                fn = getattr(self._g, name)

                def call(*a, **k):
                    self._g.wait_ge(sem["sPl"], pl[0])
                    ins = fn(*a, **k)
                    pl[0] += 1
                    ins.then_inc(sem["sPl"], 1)
                    return ins
                return call

        pA = ExitStack()
        with pA:
            ln1T = pA.enter_context(nc.sbuf_tensor("ln1T", [128, NI * CTX], BF16))
            ccs = pA.enter_context(nc.sbuf_tensor("ccs", [128, CTX], F32))
            sss = pA.enter_context(nc.sbuf_tensor("sss", [128, CTX], F32))
            wbuf = [pA.enter_context(
                nc.sbuf_tensor(f"wbuf{j}", [128, 16 * 1024], BF16))
                for j in range(2)]

            def emit_wload(g, li):
                wt, oh = WLOADS[li]
                ws = sem["sWa" if li % 2 == 0 else "sWb"]
                for i in range(NI):
                    g.dma_start(
                        wbuf[li % 2][:, i * 1024:(i + 1) * 1024],
                        wt[i * 128:(i + 1) * 128,
                           oh * 1024:(oh + 1) * 1024]).then_inc(ws, 16)

            p0 = ExitStack()
            with p0:
                xb = [p0.enter_context(nc.sbuf_tensor(f"xb{j}", [128, D], F32))
                      for j in range(4)]
                xsem = ["sXa", "sXb", "sXc", "sXd"]
                lnb = [p0.enter_context(nc.sbuf_tensor(f"lnb{j}", [128, D], BF16))
                       for j in range(2)]
                ptr = [p0.enter_context(nc.psum_tensor(f"ptr{j}", [128, 1024], BF16))
                       for j in range(2)]
                bar_n[0] += 1
                with nc.Block() as blk:
                    @blk.sync
                    def _(s):
                        barrier(s)
                        s.dma_start(idm[:], idm_in).then_inc(sem["sMisc"], 16)
                        s.dma_start(ccs[:], cc_in).then_inc(sem["sMisc"], 16)
                        s.dma_start(sss[:], ss_in).then_inc(sem["sMisc"], 16)
                        for t in range(NT0):
                            if t >= 4:
                                s.wait_ge(sem["sLn1"], t - 3)
                            s.dma_start(xb[t % 4][:],
                                        x_ctx[t * 128:(t + 1) * 128, :]
                                        ).then_inc(sem[xsem[t % 4]], 16)

                    @blk.vector
                    def _(vraw):
                        barrier(vraw)
                        v = _VW(vraw)
                        v.memset(eps_t[:], EPS)
                        v.memset(ones1[:], 1.0)
                        def evac(t):
                            for r in range(2):
                                v.wait_ge(sem["sTp"], 2 * t + r + 1)
                                dst = ln1T[:, :].rearrange(
                                    "p (i s) -> p i s", i=NI
                                )[:, 8 * r: 8 * r + 8, t * 128:(t + 1) * 128]
                                srcv = ptr[r][:, :].rearrange(
                                    "p (i s) -> p i s", i=8)
                                v.tensor_copy(dst, srcv)
                                v.sig(sem["sLnT"])
                        for t in range(NT0):
                            j = t % 4
                            v.wait_ge(sem[xsem[j]], 16 * (t // 4 + 1))
                            for k in range(4):
                                v.bn_stats(stats[:, j * 24 + k * 6:
                                                 j * 24 + (k + 1) * 6],
                                           xb[j][:, k * 512:(k + 1) * 512])
                            v.bn_aggr(aggr[:, j * 2: j * 2 + 2],
                                      stats[:, j * 24: j * 24 + 24])
                            v.sig(sem["sStat"])
                            if t >= 1:
                                evac(t - 1)
                            v.wait_ge(sem["sRstd"], 2 * (t + 1))
                            v.tensor_scalar(nmr[:, j: j + 1],
                                            aggr[:, j * 2: j * 2 + 1],
                                            rstd[:, j: j + 1], -1.0,
                                            OP.mult, OP.mult)
                            v.sig(sem["sNmr"])
                        evac(NT0 - 1)

                    @blk.scalar
                    def _(scraw):
                        barrier(scraw)
                        sc = _AW(scraw)
                        for t in range(NT0):
                            j = t % 4
                            sc.wait_ge(sem["sStat"], t + 1)
                            sc.activation(rstdl[:, j: j + 1],
                                          aggr[:, j * 2 + 1: j * 2 + 2],
                                          AF.Ln, bias=eps_t[:])
                            sc.sig(sem["sRstd"])
                            sc.wait_ge(sem["sRstd"], 2 * t + 1)
                            sc.activation(rstd[:, j: j + 1], rstdl[:, j: j + 1],
                                          AF.Exp, scale=-0.5)
                            sc.sig(sem["sRstd"])
                            sc.wait_ge(sem["sNmr"], t + 1)
                            sc.activation(lnb[t % 2][:], xb[j][:], AF.Identity,
                                          scale=rstd[:, j: j + 1],
                                          bias=nmr[:, j: j + 1])
                            sc.sig(sem["sLn1"])

                    @blk.tensor
                    def _(tn):
                        barrier(tn)
                        tn.wait_ge(sem["sMisc"], 48)
                        for t in range(NT0):
                            tn.wait_ge(sem["sLn1"], t + 1)
                            for r in range(2):
                                if 2 * t + r >= 2:
                                    tn.wait_ge(sem["sLnT"], 2 * t + r - 1)
                                ins = None
                                for jj in range(8):
                                    ic = 8 * r + jj
                                    ins = tn.transpose(
                                        ptr[r][:, jj * 128:(jj + 1) * 128],
                                        lnb[t % 2][:, ic * 128:(ic + 1) * 128],
                                        idm[:])
                                ins.then_inc(sem["sTp"], 1)

                    @blk.gpsimd
                    def _(g):
                        barrier(g)
                        emit_wload(g, 0)
                        emit_wload(g, 1)

            p1 = ExitStack()
            with p1:
                e1b = [p1.enter_context(nc.sbuf_tensor(f"e1b{j}", [128, 512], F32))
                       for j in range(2)]
                e2b = [p1.enter_context(nc.sbuf_tensor(f"e2b{j}", [128, 512], F32))
                       for j in range(2)]
                tt1 = [p1.enter_context(nc.sbuf_tensor(f"tt1{j}", [128, 512], BF16))
                       for j in range(2)]
                tt2 = [p1.enter_context(nc.sbuf_tensor(f"tt2{j}", [128, 512], BF16))
                       for j in range(2)]
                stg = [p1.enter_context(nc.sbuf_tensor(f"stg{j}", [128, 2048], BF16))
                       for j in range(2)]
                pool = p1.enter_context(nc.psum_tensor("pool1", [128, 4096], F32))
                bar_n[0] += 1

                with nc.Block() as blk:
                    @blk.gpsimd
                    def _(g):
                        barrier(g)
                        for li in range(2, len(WLOADS)):
                            g.wait_ge(sem["sPEh"], li - 1)
                            emit_wload(g, li)

                    @blk.tensor
                    def _(tn):
                        barrier(tn)
                        grp_done = 0
                        li = 0
                        tile = 0
                        slot_end = [0, 0]
                        for name, schunks, qoff, rope in SWEEPS:
                            nsc = len(schunks)
                            for oh in range(2):
                                tn.wait_ge(sem["sWa" if li % 2 == 0 else "sWb"],
                                           16 * NI * (li // 2 + 1))
                                for ol in range(8):
                                    slot = ol % 2
                                    if slot_end[slot] > 0:
                                        tn.wait_ge(sem["sEv"], slot_end[slot])
                                    for i in range(NI):
                                        lhs = wbuf[li % 2][:, i * 1024 + ol * 128:
                                                           i * 1024 + (ol + 1) * 128]
                                        for k, sb in enumerate(schunks):
                                            roff = i * CTX + qoff + sb * 512
                                            ins = tn.matmul(
                                                pool[:, slot * 2048 + k * 512:
                                                     slot * 2048 + (k + 1) * 512],
                                                lhs,
                                                ln1T[:, roff: roff + 512],
                                                start=(i == 0),
                                                stop=(i == NI - 1))
                                    ins.then_inc(sem["sPs"], 1)
                                    grp_done += 1
                                    tile += nsc
                                    slot_end[slot] = tile
                                tn.wait_ge(sem["sPs"], grp_done)
                                tn.sem_inc(sem["sPEh"], 1)
                                li += 1

                    @blk.scalar
                    def _(scraw):
                        barrier(scraw)
                        sc = _AW(scraw)
                        grp = 0
                        tile = 0
                        rtile = 0
                        for name, schunks, qoff, rope in SWEEPS:
                            nsc = len(schunks)
                            for oh in range(2):
                                for ol in range(8):
                                    o = oh * 8 + ol
                                    slot = ol % 2
                                    sc.wait_ge(sem["sPs"], grp + 1)
                                    for k in range(nsc):
                                        ps = pool[:, slot * 2048 + k * 512:
                                                  slot * 2048 + (k + 1) * 512]
                                        if rope:
                                            jt = rtile % 2
                                            if rtile >= 2:
                                                sc.wait_ge(sem["sRope"],
                                                           rtile - 1)
                                            sc.copy(e1b[jt][:], ps)
                                            sc.copy(e2b[jt][0:64, :],
                                                    ps[64:128, :])
                                            sc.copy(e2b[jt][64:128, :],
                                                    ps[0:64, :])
                                            rtile += 1
                                        else:
                                            if k == 0 and grp >= 2:
                                                sc.wait_ge(
                                                    sem["sSpQa" if grp % 2 == 0
                                                        else "sSpQb"],
                                                    16 * (grp // 2))
                                            sc.copy(stg[o % 2][:, k * 512:
                                                              (k + 1) * 512], ps)
                                        sc.sig(sem["sEv"])
                                        tile += 1
                                    if not rope:
                                        sc.sig(sem["sStgV"])
                                    grp += 1

                    @blk.vector
                    def _(vraw):
                        barrier(vraw)
                        v = _VW(vraw)
                        tile = 0
                        rtile = 0
                        sgrp = 0
                        for name, schunks, qoff, rope in SWEEPS:
                            nsc = len(schunks)
                            for oh in range(2):
                                for ol in range(8):
                                    o = oh * 8 + ol
                                    if not rope:
                                        tile += nsc
                                        sgrp += 1
                                        continue
                                    for k, sb in enumerate(schunks):
                                        jt = rtile % 2
                                        v.wait_ge(sem["sEv"], tile + 1)
                                        col0 = qoff + sb * 512
                                        v.tensor_mul(tt1[jt][:], e1b[jt][:],
                                                     ccs[:, col0: col0 + 512])
                                        v.tensor_mul(tt2[jt][:], e2b[jt][:],
                                                     sss[:, col0: col0 + 512])
                                        if k == 0 and sgrp >= 2:
                                            v.wait_ge(
                                                sem["sSpQa" if sgrp % 2 == 0
                                                    else "sSpQb"],
                                                16 * (sgrp // 2))
                                        v.tensor_add(
                                            stg[o % 2][:, k * 512:(k + 1) * 512],
                                            tt1[jt][:], tt2[jt][:])
                                        v.sig(sem["sRope"])
                                        tile += 1
                                        rtile += 1
                                    v.sig(sem["sStg"])
                                    sgrp += 1

                    @blk.sync
                    def _(s):
                        barrier(s)
                        if ln1t_sp is not None:
                            s.dma_start(ln1t_sp, ln1T[:]).then_inc(
                                sem["sMisc"], 16)
                            s.wait_ge(sem["sMisc"], 64)
                        sgrp = 0
                        rgrp = 0
                        vgrp = 0
                        for name, schunks, qoff, rope in SWEEPS:
                            nsc = len(schunks)
                            for oh in range(2):
                                for ol in range(8):
                                    o = oh * 8 + ol
                                    if rope:
                                        rgrp += 1
                                        s.wait_ge(sem["sStg"], rgrp)
                                    else:
                                        vgrp += 1
                                        s.wait_ge(sem["sStgV"], vgrp)
                                    if name == "q":
                                        dst = q_sp[o * 128:(o + 1) * 128, :]
                                        src = stg[o % 2][:, 0:2048]
                                    elif name in ("ka", "va"):
                                        base = k_sp if name == "ka" else vT_sp
                                        dst = base[o * 128:(o + 1) * 128, 0:1536]
                                        src = stg[o % 2][:, 0:1536]
                                    else:
                                        base = k_sp if name == "kb" else vT_sp
                                        dst = base[o * 128:(o + 1) * 128,
                                                   1536:2560]
                                        src = stg[o % 2][:, 0:1024]
                                    s.dma_start(dst, src).then_inc(
                                        sem["sSpQa" if sgrp % 2 == 0
                                            else "sSpQb"], 16)
                                    sgrp += 1
                        s.wait_ge(sem["sSpQa"], 16 * ((NP1_G + 1) // 2))
                        s.wait_ge(sem["sSpQb"], 16 * (NP1_G // 2))

        p2o = ExitStack()
        kring = p2o.enter_context(nc.sbuf_tensor("kring", [128, 16 * 1024], BF16))
        vring = p2o.enter_context(nc.sbuf_tensor("vring", [128, 8 * 2048], BF16))
        qring = p2o.enter_context(nc.sbuf_tensor("qring", [128, 16 * 512], BF16))
        msk = p2o.enter_context(nc.sbuf_tensor("msk_s", [128, 5 * 640], BF16))
        k_spv = k_sp.rearrange("(o p) t -> p o t", p=128)
        v_spv = v_sp.rearrange("(s p) f -> p s f", p=128)
        q_spv = q_sp.rearrange("(o p) t -> p o t", p=128)

        def ring_kq(g, qb):
            for o0 in range(0, 16, 4):
                g.dma_start(
                    kring[:, o0 * 1024:(o0 + 4) * 1024].rearrange(
                        "p (o c) -> p o c", c=1024),
                    k_spv[:, o0: o0 + 4, qb * 512: qb * 512 + 1024]
                ).then_inc(sem["sRing"], 16)
            for o0 in range(0, 16, 8):
                g.dma_start(
                    qring[:, o0 * 512:(o0 + 8) * 512].rearrange(
                        "p (o c) -> p o c", c=512),
                    q_spv[:, o0: o0 + 8, qb * 512:(qb + 1) * 512]
                ).then_inc(sem["sRing"], 16)

        def ring_v(g, qb):
            for j0 in range(0, 8, 2):
                g.dma_start(
                    vring[:, j0 * 2048:(j0 + 2) * 2048].rearrange(
                        "p (j f) -> p j f", f=2048),
                    v_spv[:, qb * 4 + j0: qb * 4 + j0 + 2, :]
                ).then_inc(sem["sRing"], 16)

        if DO_V:
            p1t = ExitStack()
            with p1t:
                vtl = [p1t.enter_context(
                    nc.sbuf_tensor(f"vtl{j}", [128, 16 * 512], BF16))
                    for j in range(2)]
                vstage = [p1t.enter_context(
                    nc.sbuf_tensor(f"vstage{j}", [128, 2048], BF16))
                    for j in range(2)]
                ptpv = [p1t.enter_context(
                    nc.psum_tensor(f"ptpv{j}", [128, 2048], BF16))
                    for j in range(2)]
                bar_n[0] += 1
                with nc.Block() as blk:
                    @blk.gpsimd
                    def _(g):
                        barrier(g)
                        for sg4 in range(5):
                            if sg4 >= 2:
                                g.wait_ge(sem["sTpV"], 4 * (sg4 - 1))
                            vs = sem["sVta" if sg4 % 2 == 0 else "sVtb"]
                            for o in range(16):
                                g.dma_start(
                                    vtl[sg4 % 2][:, o * 512:(o + 1) * 512],
                                    vT_sp[o * 128:(o + 1) * 128,
                                          sg4 * 512:(sg4 + 1) * 512]
                                ).then_inc(vs, 16)
                        g.dma_start(msk[:], msk_in).then_inc(sem["sMsk"], 16)
                        ring_kq(g, 0)
                        g.wait_ge(sem["sSpVa"], 16 * 4)
                        g.wait_ge(sem["sSpVb"], 16 * 4)
                        ring_v(g, 0)

                    @blk.sync
                    def _(s):
                        barrier(s)
                        for st in range(20):
                            s.wait_ge(sem["sEvV"], st + 1)
                            s.dma_start(
                                v_sp[st * 128:(st + 1) * 128, :],
                                vstage[st % 2][:]
                            ).then_inc(
                                sem["sSpVa" if st % 2 == 0 else "sSpVb"], 16)
                        s.wait_ge(sem["sSpVa"], 16 * 10)
                        s.wait_ge(sem["sSpVb"], 16 * 10)

                    @blk.tensor
                    def _(tn):
                        barrier(tn)
                        for sg4 in range(5):
                            tn.wait_ge(sem["sVta" if sg4 % 2 == 0 else "sVtb"],
                                       16 * 16 * (sg4 // 2 + 1))
                            for st4 in range(4):
                                st = sg4 * 4 + st4
                                if st >= 2:
                                    tn.wait_ge(sem["sEvV"], st - 1)
                                ins = None
                                for o in range(16):
                                    ins = tn.transpose(
                                        ptpv[st % 2][:, o * 128:(o + 1) * 128],
                                        vtl[sg4 % 2][:, o * 512 + st4 * 128:
                                                     o * 512 + (st4 + 1) * 128],
                                        idm[:])
                                ins.then_inc(sem["sTpV"], 1)

                    @blk.scalar
                    def _(scraw):
                        barrier(scraw)
                        sc = _AW(scraw)
                        for st in range(20):
                            sc.wait_ge(sem["sTpV"], st + 1)
                            if st >= 2:
                                sc.wait_ge(
                                    sem["sSpVa" if st % 2 == 0 else "sSpVb"],
                                    16 * (st // 2))
                            sc.copy(vstage[st % 2][:], ptpv[st % 2][:])
                            sc.sig(sem["sEvV"])

                    @blk.vector
                    def _(vraw):
                        barrier(vraw)

        if MAX_PHASE < 2:
            return nc
        p2 = ExitStack()
        with p2:
            wo_sb = p2.enter_context(nc.sbuf_tensor("wo_sb", [128, 16 * 2048], BF16))
            expb = [p2.enter_context(nc.sbuf_tensor(f"expb{j}", [128, 640], BF16))
                    for j in range(NEXPB)]
            sums = p2.enter_context(nc.sbuf_tensor("sums", [128, NEXPB], F32))
            rec = p2.enter_context(nc.sbuf_tensor("rec", [128, NEXPB], F32))
            expT = [p2.enter_context(nc.sbuf_tensor(f"expT{j}", [128, 2560], BF16))
                    for j in range(2)]
            outT = p2.enter_context(nc.sbuf_tensor("outT", [128, 16 * 512], BF16))
            xh = [p2.enter_context(nc.sbuf_tensor(f"xh{j}", [128, 1024], F32))
                  for j in range(3)]
            ybuf = p2.enter_context(nc.sbuf_tensor("ybuf", [128, 2048], F32))
            ln2b = p2.enter_context(nc.sbuf_tensor("ln2b", [128, 2048], BF16))
            pool = p2.enter_context(nc.psum_tensor("pool2", [128, 2048], F32))
            ptp = [p2.enter_context(nc.psum_tensor(f"ptp{j}", [128, 1024], BF16))
                   for j in range(2)]
            ppv = [p2.enter_context(nc.psum_tensor(f"ppv{j}", [128, 512], F32))
                   for j in range(2)]
            bar_n[0] += 1

            RL = 10

            def eidx(qb, h, qc):
                return qb * 64 + h * 4 + qc

            def mask_cols(qb, qc, part):
                base = (qc + 1) * 640 if qb == 0 else 0
                w0 = 512 if qb == 0 else 128
                if part == 0:
                    return msk[:, base: base + w0]
                return msk[:, base + 512: base + 640]

            with nc.Block() as blk:
                @blk.sync
                def _(s):
                    barrier(s)
                    nx = 0
                    nsp = 0
                    for qb in range(NQB):
                        for qc in range(4):
                            sg = qb * 4 + qc
                            for hf in range(2):
                                if nx >= 3:
                                    s.wait_ge(sem["sXhF"], nx - 2)
                                s.dma_start(
                                    xh[nx % 3][:],
                                    x_ctx[WIN + sg * 128: WIN + (sg + 1) * 128,
                                          hf * 1024:(hf + 1) * 1024]
                                ).then_inc(
                                    sem["sXha" if nx % 2 == 0 else "sXhb"], 16)
                                nx += 1
                            for hf in range(2):
                                s.wait_ge(sem["sYd"], sg * 2 + hf + 1)
                                s.dma_start(y_sp[sg * 128:(sg + 1) * 128,
                                                 hf * 1024:(hf + 1) * 1024],
                                            ybuf[:, hf * 1024:(hf + 1) * 1024]
                                            ).then_inc(
                                    sem["sYspA" if hf == 0 else "sYspB"], 16)
                                nsp += 1
                            s.wait_ge(sem["sLn2d"], sg + 1)
                            s.dma_start(l2_sp[sg * 128:(sg + 1) * 128, :],
                                        ln2b[:]).then_inc(sem["sL2sp"], 16)
                            nsp += 1
                    s.wait_ge(sem["sYspA"], 16 * 16)
                    s.wait_ge(sem["sYspB"], 16 * 16)
                    s.wait_ge(sem["sL2sp"], 16 * 16)

                @blk.gpsimd
                def _(g):
                    barrier(g)
                    if not DO_V:
                        g.dma_start(msk[:], msk_in).then_inc(sem["sMsk"], 16)
                        ring_kq(g, 0)
                        ring_v(g, 0)
                    wo_tv = wo_t.rearrange("(i p) f -> p i f", p=128)
                    for i0 in range(0, 16, 4):
                        g.dma_start(
                            wo_sb[:, i0 * 2048:(i0 + 4) * 2048].rearrange(
                                "p (i f) -> p i f", f=2048),
                            wo_tv[:, i0: i0 + 4, :]
                        ).then_inc(sem["sWo2"], 16)
                    for qb in range(1, NQB):
                        g.wait_ge(sem["sQbPE"], qb)
                        ring_kq(g, qb)
                        ring_v(g, qb)

                @blk.tensor
                def _(tn):
                    barrier(tn)
                    tn.wait_ge(sem["sMsk"], 16)
                    for qb in range(NQB):
                        tn.wait_ge(sem["sRing"], 16 * RL * (qb + 1))
                        tn.wait_ge(sem["sWoEv"], 8 * qb)
                        for step in range(66):
                            if step < 64:
                                h, qc = divmod(step, 4)
                                ei = eidx(qb, h, qc)
                                slot = ei % 2
                                if ei >= 2:
                                    tn.wait_ge(sem["sExp"], ei - 1)
                                c0 = slot * 1024
                                mw = 512 if qb == 0 else 128
                                tn.matmul(pool[:, c0: c0 + mw], idm[:],
                                          mask_cols(qb, qc, 0),
                                          start=True, stop=False)
                                tn.matmul(pool[:, c0 + 512: c0 + 640],
                                          idm[:], mask_cols(qb, qc, 1),
                                          start=True, stop=False)
                                qt = qring[:, h * 512 + qc * 128:
                                           h * 512 + (qc + 1) * 128]
                                kb0 = h * 1024 + qc * 128
                                tn.matmul(pool[:, c0 + 128: c0 + 512], qt,
                                          kring[:, kb0 + 128: kb0 + 512],
                                          start=False, stop=False)
                                tn.matmul(pool[:, c0: c0 + 128], qt,
                                          kring[:, kb0: kb0 + 128],
                                          start=False, stop=True)
                                tn.matmul(pool[:, c0 + 512: c0 + 640], qt,
                                          kring[:, kb0 + 512: kb0 + 640],
                                          start=False, stop=True
                                          ).then_inc(sem["sPs"], 1)
                            tst = step - 2
                            if tst >= 0:
                                hh, qc = divmod(tst, 4)
                                ei = eidx(qb, hh, qc)
                                tp_slot = ei % 2
                                eb = ei % 4
                                tn.wait_ge(sem["sExpN"], ei + 1)
                                if ei >= 2:
                                    tn.wait_ge(sem["sExpT"], ei - 1)
                                ins = None
                                for j in range(5):
                                    ins = tn.transpose(
                                        ptp[tp_slot][:, j * 128:(j + 1) * 128],
                                        expb[eb][:, j * 128:(j + 1) * 128],
                                        idm[:])
                                ins.then_inc(sem["sTpE"], 1)
                                if qc == 3:
                                    gidx = qb * 16 + hh
                                    tn.wait_ge(sem["sExpT"],
                                               eidx(qb, hh, 3) + 1)
                                    if gidx >= 2:
                                        tn.wait_ge(sem["sOutT"], gidx - 1)
                                    pv = ppv[gidx % 2]
                                    ins = None
                                    for qx in range(4):
                                        for jc in range(qx, qx + 5):
                                            vt = vring[:, jc * 2048 + hh * 128:
                                                       jc * 2048 + (hh + 1) * 128]
                                            col = qx * 512 + jc * 128
                                            ins = tn.matmul(
                                                pv[:, qx * 128:(qx + 1) * 128],
                                                vt,
                                                expT[hh % 2][:, col: col + 128],
                                                start=(jc == qx),
                                                stop=(jc == qx + 4))
                                    ins.then_inc(sem["sPV"], 1)
                        tn.wait_ge(sem["sPV"], 16 * (qb + 1))
                        tn.sem_inc(sem["sQbPE"], 1)
                        if qb == 0:
                            tn.wait_ge(sem["sWo2"], 4 * 16)
                        tn.wait_ge(sem["sOutT"], (qb + 1) * 16)
                        tn.wait_ge(sem["sExp"], (qb + 1) * 64)
                        for qc in range(4):
                            for hf in range(2):
                                widx = (qb * 4 + qc) * 2 + hf
                                if widx >= 2:
                                    tn.wait_ge(sem["sWoEv"], widx - 1)
                                ins = None
                                for hx in range(H):
                                    lhs = outT[:, hx * 512 + qc * 128:
                                               hx * 512 + (qc + 1) * 128]
                                    for col in range(2):
                                        w0c = hx * 2048 + hf * 1024 + col * 512
                                        ins = tn.matmul(
                                            pool[:, hf * 1024 + col * 512:
                                                 hf * 1024 + (col + 1) * 512],
                                            lhs, wo_sb[:, w0c: w0c + 512],
                                            start=(hx == 0), stop=(hx == H - 1))
                                ins.then_inc(sem["sWoPE"], 1)

                @blk.scalar
                def _(scraw):
                    barrier(scraw)
                    sc = _AW(scraw)
                    for qb in range(NQB):
                        def outT_copy(h):
                            gidx = qb * 16 + h
                            if h == 0:
                                sc.wait_ge(sem["sWoPE"], 8 * qb)
                            sc.wait_ge(sem["sPV"], gidx + 1)
                            scraw.copy(outT[:, h * 512:(h + 1) * 512],
                                       ppv[gidx % 2][:]
                                       ).then_inc(sem["sOutT"], 1)

                        for h in range(H):
                            for qc in range(4):
                                ei = eidx(qb, h, qc)
                                ps_slot = ei % 2
                                eb = ei % NEXPB
                                sc.wait_ge(sem["sPs"], NP1_G + ei + 1)
                                if ei >= NEXPB:
                                    sc.wait_ge(sem["sTpE"], ei - NEXPB + 1)
                                scraw.activation(
                                    expb[eb][:],
                                    pool[:, ps_slot * 1024:
                                         ps_slot * 1024 + 640],
                                    AF.Exp, scale=SCALE,
                                    accum_out=sums[:, eb: eb + 1]
                                ).then_inc(sem["sExp"], 1)
                            if h >= 1:
                                outT_copy(h - 1)
                        outT_copy(H - 1)
                        for qc in range(4):
                            sg = qb * 4 + qc
                            j = sg % 2
                            sc.wait_ge(sem["sStat"], NT0 + sg + 1)
                            sc.activation(rstdl[:, j: j + 1],
                                          aggr[:, j * 2 + 1: j * 2 + 2],
                                          AF.Ln, bias=eps_t[:])
                            sc.sig(sem["sRstd"])
                            sc.wait_ge(sem["sRstd"], 2 * NT0 + 2 * sg + 1)
                            sc.activation(rstd[:, j: j + 1], rstdl[:, j: j + 1],
                                          AF.Exp, scale=-0.5)
                            sc.sig(sem["sRstd"])
                            sc.wait_ge(sem["sNmr"], NT0 + sg + 1)
                            if sg >= 1:
                                sc.wait_ge(sem["sL2sp"], 16 * sg)
                            sc.activation(ln2b[:], ybuf[:], AF.Identity,
                                          scale=rstd[:, j: j + 1],
                                          bias=nmr[:, j: j + 1])
                            sc.sig(sem["sLn2d"])

                @blk.vector
                def _(vraw):
                    barrier(vraw)
                    v = _VW(vraw)
                    def evacT(qb, ei2):
                        hh, qcc = divmod(ei2 - qb * 64, 4)
                        if qcc == 0:
                            gidx2 = qb * 16 + hh
                            if gidx2 >= 2:
                                v.wait_ge(sem["sPV"], gidx2 - 1)
                        v.wait_ge(sem["sTpE"], ei2 + 1)
                        v.wait_ge(sem["sDv"], dv[0])
                        vraw.tensor_copy(expT[(ei2 // 4) % 2][
                            :, (ei2 % 4) * 640:(ei2 % 4 + 1) * 640],
                            ptp[ei2 % 2][:, 0:640]
                        ).then_inc(sem["sExpT"], 1)

                    for qb in range(NQB):
                        for h in range(H):
                            for qc in range(4):
                                ei = eidx(qb, h, qc)
                                eb = ei % NEXPB
                                v.wait_ge(sem["sExp"], ei + 1)
                                v.reciprocal(rec[:, eb: eb + 1],
                                             sums[:, eb: eb + 1])
                                v.wait_ge(sem["sDv"], dv[0])
                                vraw.tensor_scalar(expb[eb][:], expb[eb][:],
                                                   rec[:, eb: eb + 1], None,
                                                   OP.mult
                                                   ).then_inc(sem["sExpN"], 1)
                                if ei - 2 >= qb * 64:
                                    evacT(qb, ei - 2)
                        evacT(qb, qb * 64 + 62)
                        evacT(qb, qb * 64 + 63)
                        for qc in range(4):
                            sg = qb * 4 + qc
                            j = sg % 2
                            if sg >= 1:
                                v.wait_ge(sem["sLn2d"], sg)
                                v.wait_ge(sem["sYspA"], 16 * sg)
                                v.wait_ge(sem["sYspB"], 16 * sg)
                            for hf in range(2):
                                widx = sg * 2 + hf
                                v.wait_ge(sem["sWoPE"], widx + 1)
                                v.wait_ge(sem["sDv"], dv[0])
                                vraw.tensor_copy(
                                    ybuf[:, hf * 1024:(hf + 1) * 1024],
                                    pool[:, hf * 1024:(hf + 1) * 1024]
                                ).then_inc(sem["sWoEv"], 1)
                            for hf in range(2):
                                widx = sg * 2 + hf
                                v.wait_ge(sem["sXha" if widx % 2 == 0
                                              else "sXhb"],
                                          16 * (widx // 2 + 1))
                                v.wait_ge(sem["sWoEv"], sg * 2 + 2)
                                v.wait_ge(sem["sDv"], dv[0])
                                vraw.tensor_add(
                                    ybuf[:, hf * 1024:(hf + 1) * 1024],
                                    ybuf[:, hf * 1024:(hf + 1) * 1024],
                                    xh[widx % 3][:]
                                ).then_inc(sem["sXhF"], 1)
                                for k in range(2):
                                    v.wait_ge(sem["sXhF"], widx + 1)
                                    v.bn_stats(
                                        stats[:, j * 24 + (hf * 2 + k) * 6:
                                              j * 24 + (hf * 2 + k + 1) * 6],
                                        ybuf[:, hf * 1024 + k * 512:
                                             hf * 1024 + (k + 1) * 512])
                                v.sig(sem["sYd"])
                            v.bn_aggr(aggr[:, j * 2: j * 2 + 2],
                                      stats[:, j * 24: j * 24 + 24])
                            v.sig(sem["sStat"])
                            v.wait_ge(sem["sRstd"], 2 * NT0 + 2 * sg + 2)
                            v.tensor_scalar(nmr[:, j: j + 1],
                                            aggr[:, j * 2: j * 2 + 1],
                                            rstd[:, j: j + 1], -1.0,
                                            OP.mult, OP.mult)
                            v.sig(sem["sNmr"])

        p2o.close()

        if MAX_PHASE < 3:
            return nc
        pD = ExitStack()
        ctx.enter_context(pD)
        w1q0 = pD.enter_context(nc.sbuf_tensor("w1q0", [128, 64 * 512], BF16))
        bias1_s = pD.enter_context(nc.sbuf_tensor("bias1_s", [1, 1024], BF16))
        pC = ExitStack()
        with pC:
            ln2T = pC.enter_context(nc.sbuf_tensor("ln2T", [128, NI * TOWN], BF16))
            w0b = [pC.enter_context(
                nc.sbuf_tensor(f"w0b{j}", [128, 16 * 1024], BF16))
                for j in range(2)]

            p3a = ExitStack()
            with p3a:
                l2b = [p3a.enter_context(nc.sbuf_tensor(f"l2b{j}", [128, D], BF16))
                       for j in range(2)]
                ptr = [p3a.enter_context(nc.psum_tensor(f"ptr3{j}", [128, 1024], BF16))
                       for j in range(2)]
                bar_n[0] += 1
                with nc.Block() as blk:
                    @blk.sync
                    def _(s):
                        barrier(s)
                        for t in range(NI):
                            if t >= 2:
                                s.wait_ge(sem["sTp"], 2 * NT0 + 2 * (t - 1))
                            s.dma_start(l2b[t % 2][:],
                                        l2_sp[t * 128:(t + 1) * 128, :]
                                        ).then_inc(
                                sem["sL2a" if t % 2 == 0 else "sL2b"], 16)

                    @blk.tensor
                    def _(tn):
                        barrier(tn)
                        for t in range(NI):
                            tn.wait_ge(sem["sL2a" if t % 2 == 0 else "sL2b"],
                                       16 * (t // 2 + 1))
                            for r in range(2):
                                if 2 * t + r >= 2:
                                    tn.wait_ge(sem["sLnT"],
                                               2 * NT0 + 2 * t + r - 1)
                                ins = None
                                for jj in range(8):
                                    ic = 8 * r + jj
                                    ins = tn.transpose(
                                        ptr[r][:, jj * 128:(jj + 1) * 128],
                                        l2b[t % 2][:, ic * 128:(ic + 1) * 128],
                                        idm[:])
                                ins.then_inc(sem["sTp"], 1)

                    @blk.vector
                    def _(vraw):
                        barrier(vraw)
                        v = _VW(vraw)
                        for t in range(NI):
                            for r in range(2):
                                v.wait_ge(sem["sTp"], 2 * NT0 + 2 * t + r + 1)
                                dst = ln2T[:, :].rearrange(
                                    "p (i s) -> p i s", i=NI
                                )[:, 8 * r: 8 * r + 8, t * 128:(t + 1) * 128]
                                src = ptr[r][:, :].rearrange(
                                    "p (i s) -> p i s", i=8)
                                v.tensor_copy(dst, src)
                                v.sig(sem["sLnT"])

                    @blk.scalar
                    def _(scraw):
                        barrier(scraw)
                        sc = _AW(scraw)

                    @blk.gpsimd
                    def _(g):
                        barrier(g)
                        for i in range(NI):
                            g.dma_start(
                                w0b[0][:, i * 1024:(i + 1) * 1024],
                                w0_t[i * 128:(i + 1) * 128, 0:1024]
                            ).then_inc(sem["sWa"], 16)

            p3b = ExitStack()
            with p3b:
                gst = [p3b.enter_context(nc.sbuf_tensor(f"gst{j}", [128, 2048], BF16))
                       for j in range(2)]
                bias0_s = p3b.enter_context(nc.sbuf_tensor("bias0_s", [128, 64], F32))
                pool = p3b.enter_context(nc.psum_tensor("pool3", [128, 4096], F32))
                bar_n[0] += 1
                with nc.Block() as blk:
                    @blk.gpsimd
                    def _(g):
                        barrier(g)
                        for ng in range(1, 8):
                            g.wait_ge(sem["sF1"], 8 * (ng - 1))
                            ws = sem["sWa" if ng % 2 == 0 else "sWb"]
                            for i in range(NI):
                                g.dma_start(
                                    w0b[ng % 2][:, i * 1024:(i + 1) * 1024],
                                    w0_t[i * 128:(i + 1) * 128,
                                         ng * 1024:(ng + 1) * 1024]
                                ).then_inc(ws, 16)
                        g.dma_start(bias1_s[:, 0:512],
                                    bias1_r[:, 0:512]).then_inc(sem["sB1"], 16)
                        for n in range(64):
                            g.dma_start(w1q0[:, n * 512:(n + 1) * 512],
                                        w1_t[n * 128:(n + 1) * 128, 0:512]
                                        ).then_inc(sem["sW4a"], 16)

                    @blk.tensor
                    def _(tn):
                        barrier(tn)
                        tn.wait_ge(sem["sLnT"], 2 * NT0 + 2 * NI)
                        for ng in range(8):
                            tn.wait_ge(sem["sWa" if ng % 2 == 0 else "sWb"],
                                       W1P[ng % 2] + 256 * (ng // 2 + 1))
                            for nl in range(8):
                                nn = ng * 8 + nl
                                slot = nn % 2
                                if nn >= 2:
                                    tn.wait_ge(sem["sGel"], nn - 1)
                                ins = None
                                for i in range(NI):
                                    lhs = w0b[ng % 2][:, i * 1024 + nl * 128:
                                                      i * 1024 + (nl + 1) * 128]
                                    for scv in range(4):
                                        ins = tn.matmul(
                                            pool[:, slot * 2048 + scv * 512:
                                                 slot * 2048 + (scv + 1) * 512],
                                            lhs,
                                            ln2T[:, i * TOWN + scv * 512:
                                                 i * TOWN + (scv + 1) * 512],
                                            start=(i == 0), stop=(i == NI - 1))
                                ins.then_inc(sem["sF1"], 1)

                    @blk.scalar
                    def _(scraw):
                        barrier(scraw)
                        sc = _AW(scraw)
                        sc.wait_ge(sem["sMisc"], 64)
                        for nn in range(64):
                            slot = nn % 2
                            sc.wait_ge(sem["sF1"], nn + 1)
                            if nn >= 2:
                                sc.wait_ge(sem["sGa" if nn % 2 == 0 else "sGb"],
                                           16 * (nn // 2))
                            sc.activation(gst[slot][:],
                                          pool[:, slot * 2048:
                                               slot * 2048 + 2048],
                                          AF.Gelu,
                                          bias=bias0_s[:, nn: nn + 1])
                            sc.sig(sem["sGel"])

                    @blk.sync
                    def _(s):
                        barrier(s)
                        s.dma_start(bias0_s[:], bias0_t).then_inc(sem["sMisc"], 16)
                        for nn in range(64):
                            s.wait_ge(sem["sGel"], nn + 1)
                            s.dma_start(g_sp[:, nn * 2048:(nn + 1) * 2048],
                                        gst[nn % 2][:]).then_inc(
                                sem["sGa" if nn % 2 == 0 else "sGb"], 16)
                        s.wait_ge(sem["sGa"], 16 * 32)
                        s.wait_ge(sem["sGb"], 16 * 32)

                    @blk.vector
                    def _(vraw):
                        barrier(vraw)
                        v = _VW(vraw)

        if MAX_PHASE < 4:
            return nc
        p4 = ExitStack()
        with p4:
            w1q1 = p4.enter_context(
                nc.sbuf_tensor("w1q1", [128, 64 * 512], BF16))
            w1q = [w1q0, w1q1]
            gq = [p4.enter_context(nc.sbuf_tensor(f"gq{j}", [128, 64 * 256], BF16))
                  for j in range(2)]
            yh = [p4.enter_context(nc.sbuf_tensor(f"yh{j}", [128, 512], F32))
                  for j in range(2)]
            obuf = [p4.enter_context(nc.sbuf_tensor(f"obuf{j}", [128, 512], F32))
                    for j in range(2)]
            pool = p4.enter_context(nc.psum_tensor("pool4", [128, 2048], F32))
            bar_n[0] += 1

            with nc.Block() as blk:
                @blk.gpsimd
                def _(g):
                    barrier(g)
                    gv = g_sp.rearrange("p (n s) -> p n s", s=2048)
                    for hq4 in range(4):
                        for sq in range(8):
                            gqi = hq4 * 8 + sq
                            if gqi >= 2:
                                g.wait_ge(sem["sF2"], 2 * (gqi - 1))
                            gsem = sem["sGqa" if gqi % 2 == 0 else "sGqb"]
                            for grp in range(8):
                                gsrc = gv[:, grp * 8:(grp + 1) * 8,
                                          sq * 256:(sq + 1) * 256]
                                dstv = gq[gqi % 2][:, :].rearrange(
                                    "p (n s) -> p n s", s=256)
                                g.dma_start(dstv[:, grp * 8:(grp + 1) * 8, :],
                                            gsrc).then_inc(gsem, 16)
                            if sq >= 1 and hq4 < 3:
                                nq = hq4 + 1
                                ws = sem["sW4a" if nq % 2 == 0 else "sW4b"]
                                if sq == 1:
                                    g.wait_ge(sem["sF2"], 16 * hq4)
                                    g.dma_start(
                                        bias1_s[:, (nq % 2) * 512:
                                                (nq % 2) * 512 + 512],
                                        bias1_r[:, nq * 512:(nq + 1) * 512]
                                    ).then_inc(
                                        sem["sB1" if nq % 2 == 0 else "sB1b"],
                                        16)
                                n0 = (sq - 1) * 10
                                for n in range(n0, min(n0 + 10, 64)):
                                    g.dma_start(
                                        w1q[nq % 2][:, n * 512:(n + 1) * 512],
                                        w1_t[n * 128:(n + 1) * 128,
                                             nq * 512:(nq + 1) * 512]
                                    ).then_inc(ws, 16)

                @blk.sync
                def _(s):
                    barrier(s)
                    fi = 0
                    for hq4 in range(4):
                        for sq in range(8):
                            for s1 in range(2):
                                sg = sq * 2 + s1
                                if fi >= 2:
                                    s.wait_ge(sem["sOb"], fi - 1)
                                s.dma_start(yh[fi % 2][:],
                                            y_sp[sg * 128:(sg + 1) * 128,
                                                 hq4 * 512:(hq4 + 1) * 512]
                                            ).then_inc(
                                    sem["sYha" if fi % 2 == 0 else "sYhb"], 16)
                                s.wait_ge(sem["sOb"], fi + 1)
                                s.dma_start(
                                    out[sg * 128:(sg + 1) * 128,
                                        hq4 * 512:(hq4 + 1) * 512],
                                    obuf[fi % 2][:]
                                ).then_inc(
                                    sem["sOua" if fi % 2 == 0 else "sOub"], 16)
                                fi += 1
                    s.wait_ge(sem["sOua"], 16 * 32)
                    s.wait_ge(sem["sOub"], 16 * 32)

                @blk.tensor
                def _(tn):
                    barrier(tn)
                    fi = 0
                    for hq4 in range(4):
                        tn.wait_ge(sem["sB1" if hq4 % 2 == 0 else "sB1b"],
                                   16 * (hq4 // 2 + 1))
                        tn.wait_ge(sem["sW4a" if hq4 % 2 == 0 else "sW4b"],
                                   1024 * (hq4 // 2 + 1))
                        for sq in range(8):
                            gqi = hq4 * 8 + sq
                            tn.wait_ge(sem["sGqa" if gqi % 2 == 0 else "sGqb"],
                                       16 * 8 * (gqi // 2 + 1))
                            for s1 in range(2):
                                pp = pool[:, (fi % 4) * 512:
                                          (fi % 4) * 512 + 512]
                                if fi >= 4:
                                    tn.wait_ge(sem["sOb"], fi - 3)
                                for n in range(64):
                                    lhs = gq[gqi % 2][:, n * 256 + s1 * 128:
                                                      n * 256 + (s1 + 1) * 128]
                                    tn.matmul(pp, lhs,
                                              w1q[hq4 % 2][:, n * 512:
                                                           (n + 1) * 512],
                                              start=(n == 0), stop=False)
                                tn.matmul(pp, ones1[:],
                                          bias1_s[:, (hq4 % 2) * 512:
                                                  (hq4 % 2) * 512 + 512],
                                          start=False, stop=True
                                          ).then_inc(sem["sF2"], 1)
                                fi += 1

                @blk.vector
                def _(vraw):
                    barrier(vraw)
                    v = _VW(vraw)
                    fi = 0
                    for hq4 in range(4):
                        for sq in range(8):
                            for s1 in range(2):
                                v.wait_ge(sem["sF2"], fi + 1)
                                v.wait_ge(sem["sYha" if fi % 2 == 0 else "sYhb"],
                                          16 * (fi // 2 + 1))
                                if fi >= 2:
                                    v.wait_ge(sem["sOua" if fi % 2 == 0
                                                  else "sOub"],
                                              16 * (fi // 2))
                                v.tensor_add(
                                    obuf[fi % 2][:],
                                    pool[:, (fi % 4) * 512:
                                         (fi % 4) * 512 + 512],
                                    yh[fi % 2][:])
                                v.sig(sem["sOb"])
                                fi += 1

                @blk.scalar
                def _(scraw):
                    barrier(scraw)
                    sc = _AW(scraw)

    return nc


def _host_prep(inputs):
    x = np.asarray(inputs["L0_attn_x"], dtype=np.float32)
    wq = np.asarray(inputs["L0_attn_Wq"], dtype=np.float32)
    wk = np.asarray(inputs["L0_attn_Wk"], dtype=np.float32)
    wv = np.asarray(inputs["L0_attn_Wv"], dtype=np.float32)
    wo = np.asarray(inputs["L0_attn_Wo"], dtype=np.float32)
    w0 = np.asarray(inputs["L0_ffn_W0"], dtype=np.float32)
    b0 = np.asarray(inputs["L0_ffn_bias0"], dtype=np.float32)
    w1 = np.asarray(inputs["L0_ffn_W1"], dtype=np.float32)
    b1 = np.asarray(inputs["L0_ffn_bias1"], dtype=np.float32)

    perm = np.concatenate([
        h * HD + np.concatenate([np.arange(0, HD, 2), np.arange(1, HD, 2)])
        for h in range(H)])
    wq_t = np.ascontiguousarray(wq.T[:, perm])
    wk_t = np.ascontiguousarray(wk.T[:, perm])
    wv_t = np.ascontiguousarray(wv.T)
    wo_t = np.ascontiguousarray(wo.T)
    w0_t = np.ascontiguousarray(w0.T)
    w1_t = np.ascontiguousarray(w1.T)
    bias0_t = np.ascontiguousarray(b0.reshape(64, 128).T)
    bias1_r = np.ascontiguousarray(b1.reshape(1, D))

    idm = np.eye(128, dtype=BF)

    def band_mask():
        p = np.arange(128)[:, None]
        w = np.arange(640)[None, :]
        valid = (w > p) & (w <= p + WIN)
        return np.where(valid, 0.0, NEGM).astype(np.float32)

    std = band_mask()

    def per_core_inputs(core):
        q = core % 4
        b = core // 4
        r0 = q * TOWN - WIN
        if q == 0:
            xc = np.concatenate(
                [np.zeros((WIN, D), np.float32), x[b, 0:TOWN]], axis=0)
        else:
            xc = x[b, r0: r0 + TOWN + WIN]
        xc = np.ascontiguousarray(xc)

        pos = np.maximum(np.arange(r0, r0 + CTX), 0).astype(np.float32)
        inv = (1.0 / (10000.0 ** (np.arange(0, HD, 2, dtype=np.float32) / HD))
               ).astype(np.float32)
        fr = pos[None, :] * inv[:, None]
        c64 = np.cos(fr).astype(np.float32)
        s64 = np.sin(fr).astype(np.float32)
        cc = np.concatenate([c64, c64], axis=0)
        ss = np.concatenate([-s64, s64], axis=0)

        slabs = [std]
        for qc in range(4):
            m = std.copy()
            if q == 0:
                kpos = qc * 128 + np.arange(640)[None, :]
                m = np.where(kpos >= WIN, m, NEGM).astype(np.float32)
            slabs.append(m)
        msk = np.concatenate(slabs, axis=1).astype(BF)

        return dict(x_ctx=xc, wq_t=wq_t, wk_t=wk_t, wv_t=wv_t, wo_t=wo_t,
                    w0_t=w0_t, w1_t=w1_t, bias0_t=bias0_t, bias1_r=bias1_r,
                    cc=np.ascontiguousarray(cc), ss=np.ascontiguousarray(ss),
                    msk=np.ascontiguousarray(msk), idm=idm)

    return [per_core_inputs(c) for c in range(8)]


def kernel(**inputs):
    in_maps = _host_prep(inputs)
    nc = build_nc()
    res = run_bass_kernel_spmd(nc, in_maps, core_ids=list(range(8)))
    x = np.asarray(inputs["L0_attn_x"])
    full = np.empty((B, S, D), dtype=np.float32)
    for c in range(8):
        b, q = c // 4, c % 4
        full[b, q * TOWN:(q + 1) * TOWN, :] = res.results[c]["out"]
    return full


# version 13 (final)
# speedup vs baseline: 1.0533x; 1.0533x over previous
import numpy as np
import ml_dtypes
from contextlib import ExitStack

import concourse.bass as bass
import concourse.mybir as mybir
from concourse.bass_utils import run_bass_kernel_spmd

F32 = mybir.dt.float32
BF16 = mybir.dt.bfloat16
AF = mybir.ActivationFunctionType
OP = mybir.AluOpType
BF = ml_dtypes.bfloat16

B, S, D, H, HD, WIN = 2, 8192, 2048, 16, 128, 512
TOWN = 2048
CTX = TOWN + WIN
NI = D // 128
NT0 = CTX // 128
NQB = TOWN // WIN
DFF = 8192
SCALE = float(HD) ** -0.5
NEGM = -1.0e9
EPS = 1e-5

DEBUG_TAPS = ()
P1_SWEEPS = ("q", "ka", "kb", "va", "vb")
DMA_SCRATCH = 16384
NEXPB = 4
MAX_PHASE = 9


def build_nc():
    nc = bass.Bass("TRN2", target_bir_lowering=False,
                   dynamic_dma_scratch_size=DMA_SCRATCH)

    def dram_in(name, shape, dtype=F32):
        return nc.dram_tensor(name, shape, dtype, kind="ExternalInput").ap()

    def dram_tmp(name, shape, dtype):
        kind = "ExternalOutput" if name in DEBUG_TAPS else "Internal"
        return nc.dram_tensor(name, shape, dtype, kind=kind).ap()

    x_ctx = dram_in("x_ctx", [CTX, D])
    wq_t = dram_in("wq_t", [D, D])
    wk_t = dram_in("wk_t", [D, D])
    wv_t = dram_in("wv_t", [D, D])
    wo_t = dram_in("wo_t", [D, D])
    w0_t = dram_in("w0_t", [D, DFF])
    w1_t = dram_in("w1_t", [DFF, D])
    bias0_t = dram_in("bias0_t", [128, 64])
    bias1_r = dram_in("bias1_r", [1, D])
    cc_in = dram_in("cc", [128, CTX])
    ss_in = dram_in("ss", [128, CTX])
    msk_in = dram_in("msk", [128, 5 * 640], BF16)
    idm_in = dram_in("idm", [128, 128], BF16)

    q_sp = dram_tmp("q_sp", [D, TOWN], BF16)
    k_sp = dram_tmp("k_sp", [D, CTX], BF16)
    v_sp = dram_tmp("v_sp", [CTX, D], BF16)
    vT_sp = dram_tmp("vT_sp", [D, CTX], BF16)
    y_sp = dram_tmp("y_sp", [TOWN, D], F32)
    ln1t_sp = (dram_tmp("ln1t_sp", [128, NI * CTX], BF16)
               if "ln1t_sp" in DEBUG_TAPS else None)
    l2_sp = dram_tmp("l2_sp", [TOWN, D], BF16)
    g_sp = dram_tmp("g_sp", [128, 64 * TOWN], BF16)

    out = nc.dram_tensor("out", [TOWN, D], F32, kind="ExternalOutput").ap()

    ALL_SWEEPS = [("q", [0, 1, 2, 3], WIN, True),
                  ("ka", [0, 1, 2], 0, True),
                  ("kb", [3, 4], 0, True),
                  ("va", [0, 1, 2], 0, False),
                  ("vb", [3, 4], 0, False)]
    SWEEPS = [s for s in ALL_SWEEPS if s[0] in P1_SWEEPS]
    DO_V = any(nm in P1_SWEEPS for nm in ("va", "vb"))
    NP1_G = 16 * len(SWEEPS)
    P1_T = sum(16 * len(s[1]) for s in SWEEPS)
    NROPE_G = 16 * sum(1 for s in SWEEPS if s[3])

    _wmap = {"q": wq_t, "ka": wk_t, "kb": wk_t, "va": wv_t, "vb": wv_t}
    WLOADS = [(_wmap[nm], oh) for nm, _, _, _ in SWEEPS for oh in range(2)]
    W1P = [0, 0]
    for li in range(len(WLOADS)):
        W1P[li % 2] += 256
    W3P = [1024, 1024]

    ctx = ExitStack()
    with ctx:
        e = ctx.enter_context

        idm = e(nc.sbuf_tensor("idm_s", [128, 128], BF16))
        eps_t = e(nc.sbuf_tensor("eps_t", [128, 1], F32))
        ones1 = e(nc.sbuf_tensor("ones1", [1, 128], BF16))
        stats = e(nc.sbuf_tensor("stats", [128, 96], F32))
        aggr = e(nc.sbuf_tensor("aggr", [128, 8], F32))
        rstdl = e(nc.sbuf_tensor("rstdl", [128, 4], F32))
        rstd = e(nc.sbuf_tensor("rstd", [128, 4], F32))
        nmr = e(nc.sbuf_tensor("nmr", [128, 4], F32))

        semnames = [
            "sBar", "sStat", "sRstd", "sNmr", "sLn1", "sTp",
            "sLnT", "sPs", "sEv", "sRope", "sStg", "sPEh",
            "sRing", "sExp", "sExpN", "sTpE", "sExpT", "sPV", "sOutT",
            "sQbPE", "sWoPE", "sWoEv", "sYd", "sLn2d", "sGel", "sF1",
            "sOb", "sMisc", "sF2",
            "sXa", "sXb", "sWa", "sWb", "sWo2", "sXha", "sXhb",
            "sYspA", "sYspB", "sL2sp", "sL2a", "sL2b",
            "sSpQa", "sSpQb", "sSpVa", "sSpVb", "sGa", "sGb",
            "sGqa", "sGqb", "sYha", "sYhb", "sOua", "sOub", "sDv", "sAc", "sB1",
            "sStgV", "sVta", "sVtb", "sTpV", "sEvV", "sMsk", "sXc", "sXd",
            "sW4a", "sW4b", "sB1b", "sPl", "sRc", "sDg", "sXhF",
        ]
        sem = {n: e(nc.semaphore(n)) for n in semnames}

        NENG = 5
        bar_n = [0]

        def barrier(eng):
            eng.sem_inc(sem["sBar"], 1)
            eng.wait_ge(sem["sBar"], NENG * bar_n[0])

        dv = [0]

        class _VW:
            def __init__(self, v):
                self._v = v

            def wait_ge(self, *a, **k):
                return self._v.wait_ge(*a, **k)

            def sem_inc(self, *a, **k):
                return self._v.sem_inc(*a, **k)

            def sig(self, sh, k=1):
                self._v.wait_ge(sem["sDv"], dv[0])
                self._v.sem_inc(sh, k)

            def __getattr__(self, name):
                fn = getattr(self._v, name)

                def call(*a, **k):
                    self._v.wait_ge(sem["sDv"], dv[0])
                    ins = fn(*a, **k)
                    dv[0] += 1
                    ins.then_inc(sem["sDv"], 1)
                    return ins
                return call

        ac = [0]

        class _AW:
            def __init__(self, s):
                self._s = s

            def wait_ge(self, *a, **k):
                return self._s.wait_ge(*a, **k)

            def sem_inc(self, *a, **k):
                return self._s.sem_inc(*a, **k)

            def sig(self, sh, k=1):
                self._s.wait_ge(sem["sAc"], ac[0])
                self._s.sem_inc(sh, k)

            def __getattr__(self, name):
                fn = getattr(self._s, name)

                def call(*a, **k):
                    ins = fn(*a, **k)
                    ac[0] += 1
                    ins.then_inc(sem["sAc"], 1)
                    return ins
                return call

        pl = [0]

        class _PW:
            def __init__(self, g):
                self._g = g

            def wait_ge(self, *a, **k):
                return self._g.wait_ge(*a, **k)

            def sem_inc(self, *a, **k):
                return self._g.sem_inc(*a, **k)

            def sig(self, sh, k=1):
                self._g.wait_ge(sem["sPl"], pl[0])
                self._g.sem_inc(sh, k)

            def __getattr__(self, name):
                fn = getattr(self._g, name)

                def call(*a, **k):
                    self._g.wait_ge(sem["sPl"], pl[0])
                    ins = fn(*a, **k)
                    pl[0] += 1
                    ins.then_inc(sem["sPl"], 1)
                    return ins
                return call

        pA = ExitStack()
        with pA:
            ln1T = pA.enter_context(nc.sbuf_tensor("ln1T", [128, NI * CTX], BF16))
            ccs = pA.enter_context(nc.sbuf_tensor("ccs", [128, CTX], F32))
            sss = pA.enter_context(nc.sbuf_tensor("sss", [128, CTX], F32))
            wbuf = [pA.enter_context(
                nc.sbuf_tensor(f"wbuf{j}", [128, 16 * 1024], BF16))
                for j in range(2)]

            def emit_wload(g, li):
                wt, oh = WLOADS[li]
                ws = sem["sWa" if li % 2 == 0 else "sWb"]
                for i in range(NI):
                    g.dma_start(
                        wbuf[li % 2][:, i * 1024:(i + 1) * 1024],
                        wt[i * 128:(i + 1) * 128,
                           oh * 1024:(oh + 1) * 1024]).then_inc(ws, 16)

            p0 = ExitStack()
            with p0:
                xb = [p0.enter_context(nc.sbuf_tensor(f"xb{j}", [128, D], F32))
                      for j in range(4)]
                xsem = ["sXa", "sXb", "sXc", "sXd"]
                lnb = [p0.enter_context(nc.sbuf_tensor(f"lnb{j}", [128, D], BF16))
                       for j in range(2)]
                ptr = [p0.enter_context(nc.psum_tensor(f"ptr{j}", [128, 1024], BF16))
                       for j in range(2)]
                bar_n[0] += 1
                with nc.Block() as blk:
                    @blk.sync
                    def _(s):
                        barrier(s)
                        s.dma_start(idm[:], idm_in).then_inc(sem["sMisc"], 16)
                        s.dma_start(ccs[:], cc_in).then_inc(sem["sMisc"], 16)
                        s.dma_start(sss[:], ss_in).then_inc(sem["sMisc"], 16)
                        for t in range(NT0):
                            if t >= 4:
                                s.wait_ge(sem["sLn1"], t - 3)
                            s.dma_start(xb[t % 4][:],
                                        x_ctx[t * 128:(t + 1) * 128, :]
                                        ).then_inc(sem[xsem[t % 4]], 16)

                    @blk.vector
                    def _(vraw):
                        barrier(vraw)
                        v = _VW(vraw)
                        v.memset(eps_t[:], EPS)
                        v.memset(ones1[:], 1.0)
                        def evac(t):
                            for r in range(2):
                                v.wait_ge(sem["sTp"], 2 * t + r + 1)
                                dst = ln1T[:, :].rearrange(
                                    "p (i s) -> p i s", i=NI
                                )[:, 8 * r: 8 * r + 8, t * 128:(t + 1) * 128]
                                srcv = ptr[r][:, :].rearrange(
                                    "p (i s) -> p i s", i=8)
                                v.tensor_copy(dst, srcv)
                                v.sig(sem["sLnT"])
                        for t in range(NT0):
                            j = t % 4
                            v.wait_ge(sem[xsem[j]], 16 * (t // 4 + 1))
                            for k in range(4):
                                v.bn_stats(stats[:, j * 24 + k * 6:
                                                 j * 24 + (k + 1) * 6],
                                           xb[j][:, k * 512:(k + 1) * 512])
                            v.bn_aggr(aggr[:, j * 2: j * 2 + 2],
                                      stats[:, j * 24: j * 24 + 24])
                            v.sig(sem["sStat"])
                            if t >= 1:
                                evac(t - 1)
                            v.wait_ge(sem["sRstd"], 2 * (t + 1))
                            v.tensor_scalar(nmr[:, j: j + 1],
                                            aggr[:, j * 2: j * 2 + 1],
                                            rstd[:, j: j + 1], -1.0,
                                            OP.mult, OP.mult)
                            v.sig(sem["sNmr"])
                        evac(NT0 - 1)

                    @blk.scalar
                    def _(scraw):
                        barrier(scraw)
                        sc = _AW(scraw)
                        for t in range(NT0):
                            j = t % 4
                            sc.wait_ge(sem["sStat"], t + 1)
                            sc.activation(rstdl[:, j: j + 1],
                                          aggr[:, j * 2 + 1: j * 2 + 2],
                                          AF.Ln, bias=eps_t[:])
                            sc.sig(sem["sRstd"])
                            sc.wait_ge(sem["sRstd"], 2 * t + 1)
                            sc.activation(rstd[:, j: j + 1], rstdl[:, j: j + 1],
                                          AF.Exp, scale=-0.5)
                            sc.sig(sem["sRstd"])
                            sc.wait_ge(sem["sNmr"], t + 1)
                            sc.activation(lnb[t % 2][:], xb[j][:], AF.Identity,
                                          scale=rstd[:, j: j + 1],
                                          bias=nmr[:, j: j + 1])
                            sc.sig(sem["sLn1"])

                    @blk.tensor
                    def _(tn):
                        barrier(tn)
                        tn.wait_ge(sem["sMisc"], 48)
                        for t in range(NT0):
                            tn.wait_ge(sem["sLn1"], t + 1)
                            for r in range(2):
                                if 2 * t + r >= 2:
                                    tn.wait_ge(sem["sLnT"], 2 * t + r - 1)
                                ins = None
                                for jj in range(8):
                                    ic = 8 * r + jj
                                    ins = tn.transpose(
                                        ptr[r][:, jj * 128:(jj + 1) * 128],
                                        lnb[t % 2][:, ic * 128:(ic + 1) * 128],
                                        idm[:])
                                ins.then_inc(sem["sTp"], 1)

                    @blk.gpsimd
                    def _(g):
                        barrier(g)
                        emit_wload(g, 0)
                        emit_wload(g, 1)

            p1 = ExitStack()
            with p1:
                e1b = [p1.enter_context(nc.sbuf_tensor(f"e1b{j}", [128, 512], F32))
                       for j in range(2)]
                e2b = [p1.enter_context(nc.sbuf_tensor(f"e2b{j}", [128, 512], F32))
                       for j in range(2)]
                tt1 = [p1.enter_context(nc.sbuf_tensor(f"tt1{j}", [128, 512], BF16))
                       for j in range(2)]
                tt2 = [p1.enter_context(nc.sbuf_tensor(f"tt2{j}", [128, 512], BF16))
                       for j in range(2)]
                stg = [p1.enter_context(nc.sbuf_tensor(f"stg{j}", [128, 2048], BF16))
                       for j in range(2)]
                pool = p1.enter_context(nc.psum_tensor("pool1", [128, 4096], F32))
                bar_n[0] += 1

                with nc.Block() as blk:
                    @blk.gpsimd
                    def _(g):
                        barrier(g)
                        for li in range(2, len(WLOADS)):
                            g.wait_ge(sem["sPEh"], li - 1)
                            emit_wload(g, li)

                    @blk.tensor
                    def _(tn):
                        barrier(tn)
                        grp_done = 0
                        li = 0
                        tile = 0
                        slot_end = [0, 0]
                        for name, schunks, qoff, rope in SWEEPS:
                            nsc = len(schunks)
                            for oh in range(2):
                                tn.wait_ge(sem["sWa" if li % 2 == 0 else "sWb"],
                                           16 * NI * (li // 2 + 1))
                                for ol in range(8):
                                    slot = ol % 2
                                    if slot_end[slot] > 0:
                                        tn.wait_ge(sem["sEv"], slot_end[slot])
                                    for i in range(NI):
                                        lhs = wbuf[li % 2][:, i * 1024 + ol * 128:
                                                           i * 1024 + (ol + 1) * 128]
                                        for k, sb in enumerate(schunks):
                                            roff = i * CTX + qoff + sb * 512
                                            ins = tn.matmul(
                                                pool[:, slot * 2048 + k * 512:
                                                     slot * 2048 + (k + 1) * 512],
                                                lhs,
                                                ln1T[:, roff: roff + 512],
                                                start=(i == 0),
                                                stop=(i == NI - 1))
                                    ins.then_inc(sem["sPs"], 1)
                                    grp_done += 1
                                    tile += nsc
                                    slot_end[slot] = tile
                                tn.wait_ge(sem["sPs"], grp_done)
                                tn.sem_inc(sem["sPEh"], 1)
                                li += 1

                    @blk.scalar
                    def _(scraw):
                        barrier(scraw)
                        sc = _AW(scraw)
                        grp = 0
                        tile = 0
                        rtile = 0
                        for name, schunks, qoff, rope in SWEEPS:
                            nsc = len(schunks)
                            for oh in range(2):
                                for ol in range(8):
                                    o = oh * 8 + ol
                                    slot = ol % 2
                                    sc.wait_ge(sem["sPs"], grp + 1)
                                    for k in range(nsc):
                                        ps = pool[:, slot * 2048 + k * 512:
                                                  slot * 2048 + (k + 1) * 512]
                                        if rope:
                                            jt = rtile % 2
                                            if rtile >= 2:
                                                sc.wait_ge(sem["sRope"],
                                                           rtile - 1)
                                            sc.copy(e1b[jt][:], ps)
                                            sc.copy(e2b[jt][0:64, :],
                                                    ps[64:128, :])
                                            sc.copy(e2b[jt][64:128, :],
                                                    ps[0:64, :])
                                            rtile += 1
                                        else:
                                            if k == 0 and grp >= 2:
                                                sc.wait_ge(
                                                    sem["sSpQa" if grp % 2 == 0
                                                        else "sSpQb"],
                                                    16 * (grp // 2))
                                            sc.copy(stg[o % 2][:, k * 512:
                                                              (k + 1) * 512], ps)
                                        sc.sig(sem["sEv"])
                                        tile += 1
                                    if not rope:
                                        sc.sig(sem["sStgV"])
                                    grp += 1

                    @blk.vector
                    def _(vraw):
                        barrier(vraw)
                        v = _VW(vraw)
                        tile = 0
                        rtile = 0
                        sgrp = 0
                        for name, schunks, qoff, rope in SWEEPS:
                            nsc = len(schunks)
                            for oh in range(2):
                                for ol in range(8):
                                    o = oh * 8 + ol
                                    if not rope:
                                        tile += nsc
                                        sgrp += 1
                                        continue
                                    for k, sb in enumerate(schunks):
                                        jt = rtile % 2
                                        v.wait_ge(sem["sEv"], tile + 1)
                                        col0 = qoff + sb * 512
                                        v.tensor_mul(tt1[jt][:], e1b[jt][:],
                                                     ccs[:, col0: col0 + 512])
                                        v.tensor_mul(tt2[jt][:], e2b[jt][:],
                                                     sss[:, col0: col0 + 512])
                                        if k == 0 and sgrp >= 2:
                                            v.wait_ge(
                                                sem["sSpQa" if sgrp % 2 == 0
                                                    else "sSpQb"],
                                                16 * (sgrp // 2))
                                        v.tensor_add(
                                            stg[o % 2][:, k * 512:(k + 1) * 512],
                                            tt1[jt][:], tt2[jt][:])
                                        v.sig(sem["sRope"])
                                        tile += 1
                                        rtile += 1
                                    v.sig(sem["sStg"])
                                    sgrp += 1

                    @blk.sync
                    def _(s):
                        barrier(s)
                        if ln1t_sp is not None:
                            s.dma_start(ln1t_sp, ln1T[:]).then_inc(
                                sem["sMisc"], 16)
                            s.wait_ge(sem["sMisc"], 64)
                        sgrp = 0
                        rgrp = 0
                        vgrp = 0
                        for name, schunks, qoff, rope in SWEEPS:
                            nsc = len(schunks)
                            for oh in range(2):
                                for ol in range(8):
                                    o = oh * 8 + ol
                                    if rope:
                                        rgrp += 1
                                        s.wait_ge(sem["sStg"], rgrp)
                                    else:
                                        vgrp += 1
                                        s.wait_ge(sem["sStgV"], vgrp)
                                    if name == "q":
                                        dst = q_sp[o * 128:(o + 1) * 128, :]
                                        src = stg[o % 2][:, 0:2048]
                                    elif name in ("ka", "va"):
                                        base = k_sp if name == "ka" else vT_sp
                                        dst = base[o * 128:(o + 1) * 128, 0:1536]
                                        src = stg[o % 2][:, 0:1536]
                                    else:
                                        base = k_sp if name == "kb" else vT_sp
                                        dst = base[o * 128:(o + 1) * 128,
                                                   1536:2560]
                                        src = stg[o % 2][:, 0:1024]
                                    s.dma_start(dst, src).then_inc(
                                        sem["sSpQa" if sgrp % 2 == 0
                                            else "sSpQb"], 16)
                                    sgrp += 1
                        s.wait_ge(sem["sSpQa"], 16 * ((NP1_G + 1) // 2))
                        s.wait_ge(sem["sSpQb"], 16 * (NP1_G // 2))

        if DO_V:
            p1t = ExitStack()
            with p1t:
                vtl = [p1t.enter_context(
                    nc.sbuf_tensor(f"vtl{j}", [128, 16 * 512], BF16))
                    for j in range(2)]
                vstage = [p1t.enter_context(
                    nc.sbuf_tensor(f"vstage{j}", [128, 2048], BF16))
                    for j in range(2)]
                ptpv = [p1t.enter_context(
                    nc.psum_tensor(f"ptpv{j}", [128, 2048], BF16))
                    for j in range(2)]
                bar_n[0] += 1
                with nc.Block() as blk:
                    @blk.gpsimd
                    def _(g):
                        barrier(g)
                        for sg4 in range(5):
                            if sg4 >= 2:
                                g.wait_ge(sem["sTpV"], 4 * (sg4 - 1))
                            vs = sem["sVta" if sg4 % 2 == 0 else "sVtb"]
                            for o in range(16):
                                g.dma_start(
                                    vtl[sg4 % 2][:, o * 512:(o + 1) * 512],
                                    vT_sp[o * 128:(o + 1) * 128,
                                          sg4 * 512:(sg4 + 1) * 512]
                                ).then_inc(vs, 16)

                    @blk.sync
                    def _(s):
                        barrier(s)
                        for st in range(20):
                            s.wait_ge(sem["sEvV"], st + 1)
                            s.dma_start(
                                v_sp[st * 128:(st + 1) * 128, :],
                                vstage[st % 2][:]
                            ).then_inc(
                                sem["sSpVa" if st % 2 == 0 else "sSpVb"], 16)
                        s.wait_ge(sem["sSpVa"], 16 * 10)
                        s.wait_ge(sem["sSpVb"], 16 * 10)

                    @blk.tensor
                    def _(tn):
                        barrier(tn)
                        for sg4 in range(5):
                            tn.wait_ge(sem["sVta" if sg4 % 2 == 0 else "sVtb"],
                                       16 * 16 * (sg4 // 2 + 1))
                            for st4 in range(4):
                                st = sg4 * 4 + st4
                                if st >= 2:
                                    tn.wait_ge(sem["sEvV"], st - 1)
                                ins = None
                                for o in range(16):
                                    ins = tn.transpose(
                                        ptpv[st % 2][:, o * 128:(o + 1) * 128],
                                        vtl[sg4 % 2][:, o * 512 + st4 * 128:
                                                     o * 512 + (st4 + 1) * 128],
                                        idm[:])
                                ins.then_inc(sem["sTpV"], 1)

                    @blk.scalar
                    def _(scraw):
                        barrier(scraw)
                        sc = _AW(scraw)
                        for st in range(20):
                            sc.wait_ge(sem["sTpV"], st + 1)
                            if st >= 2:
                                sc.wait_ge(
                                    sem["sSpVa" if st % 2 == 0 else "sSpVb"],
                                    16 * (st // 2))
                            sc.copy(vstage[st % 2][:], ptpv[st % 2][:])
                            sc.sig(sem["sEvV"])

                    @blk.vector
                    def _(vraw):
                        barrier(vraw)

        if MAX_PHASE < 2:
            return nc
        p2 = ExitStack()
        with p2:
            kring = p2.enter_context(nc.sbuf_tensor("kring", [128, 16 * 1024], BF16))
            vring = p2.enter_context(nc.sbuf_tensor("vring", [128, 8 * 2048], BF16))
            qring = p2.enter_context(nc.sbuf_tensor("qring", [128, 16 * 512], BF16))
            wo_sb = p2.enter_context(nc.sbuf_tensor("wo_sb", [128, 16 * 2048], BF16))
            msk = p2.enter_context(nc.sbuf_tensor("msk_s", [128, 5 * 640], BF16))
            expb = [p2.enter_context(nc.sbuf_tensor(f"expb{j}", [128, 640], BF16))
                    for j in range(NEXPB)]
            sums = p2.enter_context(nc.sbuf_tensor("sums", [128, NEXPB], F32))
            rec = p2.enter_context(nc.sbuf_tensor("rec", [128, NEXPB], F32))
            expT = [p2.enter_context(nc.sbuf_tensor(f"expT{j}", [128, 2560], BF16))
                    for j in range(2)]
            outT = p2.enter_context(nc.sbuf_tensor("outT", [128, 16 * 512], BF16))
            xh = [p2.enter_context(nc.sbuf_tensor(f"xh{j}", [128, 1024], F32))
                  for j in range(3)]
            ybuf = p2.enter_context(nc.sbuf_tensor("ybuf", [128, 2048], F32))
            ln2b = p2.enter_context(nc.sbuf_tensor("ln2b", [128, 2048], BF16))
            pool = p2.enter_context(nc.psum_tensor("pool2", [128, 2048], F32))
            ptp = [p2.enter_context(nc.psum_tensor(f"ptp{j}", [128, 1024], BF16))
                   for j in range(2)]
            ppv = [p2.enter_context(nc.psum_tensor(f"ppv{j}", [128, 512], F32))
                   for j in range(2)]
            bar_n[0] += 1

            RL = 10

            def eidx(qb, h, qc):
                return qb * 64 + h * 4 + qc

            def mask_cols(qb, qc, part):
                base = (qc + 1) * 640 if qb == 0 else 0
                w0 = 512 if qb == 0 else 128
                if part == 0:
                    return msk[:, base: base + w0]
                return msk[:, base + 512: base + 640]

            with nc.Block() as blk:
                @blk.sync
                def _(s):
                    barrier(s)
                    nx = 0
                    nsp = 0
                    for qb in range(NQB):
                        for qc in range(4):
                            sg = qb * 4 + qc
                            for hf in range(2):
                                if nx >= 3:
                                    s.wait_ge(sem["sXhF"], nx - 2)
                                s.dma_start(
                                    xh[nx % 3][:],
                                    x_ctx[WIN + sg * 128: WIN + (sg + 1) * 128,
                                          hf * 1024:(hf + 1) * 1024]
                                ).then_inc(
                                    sem["sXha" if nx % 2 == 0 else "sXhb"], 16)
                                nx += 1
                            for hf in range(2):
                                s.wait_ge(sem["sYd"], sg * 2 + hf + 1)
                                s.dma_start(y_sp[sg * 128:(sg + 1) * 128,
                                                 hf * 1024:(hf + 1) * 1024],
                                            ybuf[:, hf * 1024:(hf + 1) * 1024]
                                            ).then_inc(
                                    sem["sYspA" if hf == 0 else "sYspB"], 16)
                                nsp += 1
                            s.wait_ge(sem["sLn2d"], sg + 1)
                            s.dma_start(l2_sp[sg * 128:(sg + 1) * 128, :],
                                        ln2b[:]).then_inc(sem["sL2sp"], 16)
                            nsp += 1
                    s.wait_ge(sem["sYspA"], 16 * 16)
                    s.wait_ge(sem["sYspB"], 16 * 16)
                    s.wait_ge(sem["sL2sp"], 16 * 16)

                @blk.gpsimd
                def _(g):
                    barrier(g)
                    g.dma_start(msk[:], msk_in).then_inc(sem["sMsk"], 16)

                    k_spv = k_sp.rearrange("(o p) t -> p o t", p=128)
                    v_spv = v_sp.rearrange("(s p) f -> p s f", p=128)
                    q_spv = q_sp.rearrange("(o p) t -> p o t", p=128)
                    wo_tv = wo_t.rearrange("(i p) f -> p i f", p=128)

                    def rings(qb):
                        for o0 in range(0, 16, 4):
                            g.dma_start(
                                kring[:, o0 * 1024:(o0 + 4) * 1024].rearrange(
                                    "p (o c) -> p o c", c=1024),
                                k_spv[:, o0: o0 + 4,
                                      qb * 512: qb * 512 + 1024]
                            ).then_inc(sem["sRing"], 16)
                        for j0 in range(0, 8, 2):
                            g.dma_start(
                                vring[:, j0 * 2048:(j0 + 2) * 2048].rearrange(
                                    "p (j f) -> p j f", f=2048),
                                v_spv[:, qb * 4 + j0: qb * 4 + j0 + 2, :]
                            ).then_inc(sem["sRing"], 16)
                        for o0 in range(0, 16, 8):
                            g.dma_start(
                                qring[:, o0 * 512:(o0 + 8) * 512].rearrange(
                                    "p (o c) -> p o c", c=512),
                                q_spv[:, o0: o0 + 8, qb * 512:(qb + 1) * 512]
                            ).then_inc(sem["sRing"], 16)
                    rings(0)
                    for i0 in range(0, 16, 4):
                        g.dma_start(
                            wo_sb[:, i0 * 2048:(i0 + 4) * 2048].rearrange(
                                "p (i f) -> p i f", f=2048),
                            wo_tv[:, i0: i0 + 4, :]
                        ).then_inc(sem["sWo2"], 16)
                    for qb in range(1, NQB):
                        g.wait_ge(sem["sQbPE"], qb)
                        rings(qb)

                @blk.tensor
                def _(tn):
                    barrier(tn)
                    tn.wait_ge(sem["sMsk"], 16)
                    for qb in range(NQB):
                        tn.wait_ge(sem["sRing"], 16 * RL * (qb + 1))
                        tn.wait_ge(sem["sWoEv"], 8 * qb)
                        for step in range(68):
                            if step < 64:
                                h, qc = divmod(step, 4)
                                ei = eidx(qb, h, qc)
                                slot = ei % 2
                                if ei >= 2:
                                    tn.wait_ge(sem["sExp"], ei - 1)
                                c0 = slot * 1024
                                mw = 512 if qb == 0 else 128
                                tn.matmul(pool[:, c0: c0 + mw], idm[:],
                                          mask_cols(qb, qc, 0),
                                          start=True, stop=False)
                                tn.matmul(pool[:, c0 + 512: c0 + 640],
                                          idm[:], mask_cols(qb, qc, 1),
                                          start=True, stop=False)
                                qt = qring[:, h * 512 + qc * 128:
                                           h * 512 + (qc + 1) * 128]
                                kb0 = h * 1024 + qc * 128
                                tn.matmul(pool[:, c0 + 128: c0 + 512], qt,
                                          kring[:, kb0 + 128: kb0 + 512],
                                          start=False, stop=False)
                                tn.matmul(pool[:, c0: c0 + 128], qt,
                                          kring[:, kb0: kb0 + 128],
                                          start=False, stop=True)
                                tn.matmul(pool[:, c0 + 512: c0 + 640], qt,
                                          kring[:, kb0 + 512: kb0 + 640],
                                          start=False, stop=True
                                          ).then_inc(sem["sPs"], 1)
                            tst = step - 2
                            if 0 <= tst < 64:
                                hh, qc = divmod(tst, 4)
                                ei = eidx(qb, hh, qc)
                                tp_slot = ei % 2
                                eb = ei % 4
                                tn.wait_ge(sem["sExpN"], ei + 1)
                                if ei >= 2:
                                    tn.wait_ge(sem["sExpT"], ei - 1)
                                ins = None
                                for j in range(5):
                                    ins = tn.transpose(
                                        ptp[tp_slot][:, j * 128:(j + 1) * 128],
                                        expb[eb][:, j * 128:(j + 1) * 128],
                                        idm[:])
                                ins.then_inc(sem["sTpE"], 1)
                            pvi = step - 4
                            if pvi >= 0 and pvi % 4 == 3:
                                hh = pvi // 4
                                gidx = qb * 16 + hh
                                tn.wait_ge(sem["sExpT"],
                                           eidx(qb, hh, 3) + 1)
                                if gidx >= 2:
                                    tn.wait_ge(sem["sOutT"], gidx - 1)
                                pv = ppv[gidx % 2]
                                ins = None
                                for qx in range(4):
                                    for jc in range(qx, qx + 5):
                                        vt = vring[:, jc * 2048 + hh * 128:
                                                   jc * 2048 + (hh + 1) * 128]
                                        col = qx * 512 + jc * 128
                                        ins = tn.matmul(
                                            pv[:, qx * 128:(qx + 1) * 128],
                                            vt,
                                            expT[hh % 2][:, col: col + 128],
                                            start=(jc == qx),
                                            stop=(jc == qx + 4))
                                ins.then_inc(sem["sPV"], 1)
                        tn.wait_ge(sem["sPV"], 16 * (qb + 1))
                        tn.sem_inc(sem["sQbPE"], 1)
                        if qb == 0:
                            tn.wait_ge(sem["sWo2"], 4 * 16)
                        tn.wait_ge(sem["sOutT"], (qb + 1) * 16)
                        tn.wait_ge(sem["sExp"], (qb + 1) * 64)
                        for qc in range(4):
                            for hf in range(2):
                                widx = (qb * 4 + qc) * 2 + hf
                                if widx >= 2:
                                    tn.wait_ge(sem["sWoEv"], widx - 1)
                                ins = None
                                for hx in range(H):
                                    lhs = outT[:, hx * 512 + qc * 128:
                                               hx * 512 + (qc + 1) * 128]
                                    for col in range(2):
                                        w0c = hx * 2048 + hf * 1024 + col * 512
                                        ins = tn.matmul(
                                            pool[:, hf * 1024 + col * 512:
                                                 hf * 1024 + (col + 1) * 512],
                                            lhs, wo_sb[:, w0c: w0c + 512],
                                            start=(hx == 0), stop=(hx == H - 1))
                                ins.then_inc(sem["sWoPE"], 1)

                @blk.scalar
                def _(scraw):
                    barrier(scraw)
                    sc = _AW(scraw)
                    for qb in range(NQB):
                        def outT_copy(h):
                            gidx = qb * 16 + h
                            if h == 0:
                                sc.wait_ge(sem["sWoPE"], 8 * qb)
                            sc.wait_ge(sem["sPV"], gidx + 1)
                            scraw.copy(outT[:, h * 512:(h + 1) * 512],
                                       ppv[gidx % 2][:]
                                       ).then_inc(sem["sOutT"], 1)

                        for h in range(H):
                            for qc in range(4):
                                ei = eidx(qb, h, qc)
                                ps_slot = ei % 2
                                eb = ei % NEXPB
                                sc.wait_ge(sem["sPs"], NP1_G + ei + 1)
                                if ei >= NEXPB:
                                    sc.wait_ge(sem["sTpE"], ei - NEXPB + 1)
                                scraw.activation(
                                    expb[eb][:],
                                    pool[:, ps_slot * 1024:
                                         ps_slot * 1024 + 640],
                                    AF.Exp, scale=SCALE,
                                    accum_out=sums[:, eb: eb + 1]
                                ).then_inc(sem["sExp"], 1)
                            if h >= 1:
                                outT_copy(h - 1)
                        outT_copy(H - 1)
                        for qc in range(4):
                            sg = qb * 4 + qc
                            j = sg % 2
                            sc.wait_ge(sem["sStat"], NT0 + sg + 1)
                            sc.activation(rstdl[:, j: j + 1],
                                          aggr[:, j * 2 + 1: j * 2 + 2],
                                          AF.Ln, bias=eps_t[:])
                            sc.sig(sem["sRstd"])
                            sc.wait_ge(sem["sRstd"], 2 * NT0 + 2 * sg + 1)
                            sc.activation(rstd[:, j: j + 1], rstdl[:, j: j + 1],
                                          AF.Exp, scale=-0.5)
                            sc.sig(sem["sRstd"])
                            sc.wait_ge(sem["sNmr"], NT0 + sg + 1)
                            if sg >= 1:
                                sc.wait_ge(sem["sL2sp"], 16 * sg)
                            sc.activation(ln2b[:], ybuf[:], AF.Identity,
                                          scale=rstd[:, j: j + 1],
                                          bias=nmr[:, j: j + 1])
                            sc.sig(sem["sLn2d"])

                @blk.vector
                def _(vraw):
                    barrier(vraw)
                    v = _VW(vraw)
                    def evacT(qb, ei2):
                        hh, qcc = divmod(ei2 - qb * 64, 4)
                        if qcc == 0:
                            gidx2 = qb * 16 + hh
                            if gidx2 >= 2:
                                v.wait_ge(sem["sPV"], gidx2 - 1)
                        v.wait_ge(sem["sTpE"], ei2 + 1)
                        v.wait_ge(sem["sDv"], dv[0])
                        vraw.tensor_copy(expT[(ei2 // 4) % 2][
                            :, (ei2 % 4) * 640:(ei2 % 4 + 1) * 640],
                            ptp[ei2 % 2][:, 0:640]
                        ).then_inc(sem["sExpT"], 1)

                    for qb in range(NQB):
                        for h in range(H):
                            for qc in range(4):
                                ei = eidx(qb, h, qc)
                                eb = ei % NEXPB
                                v.wait_ge(sem["sExp"], ei + 1)
                                v.reciprocal(rec[:, eb: eb + 1],
                                             sums[:, eb: eb + 1])
                                v.wait_ge(sem["sDv"], dv[0])
                                vraw.tensor_scalar(expb[eb][:], expb[eb][:],
                                                   rec[:, eb: eb + 1], None,
                                                   OP.mult
                                                   ).then_inc(sem["sExpN"], 1)
                                if ei - 2 >= qb * 64:
                                    evacT(qb, ei - 2)
                        evacT(qb, qb * 64 + 62)
                        evacT(qb, qb * 64 + 63)
                        for qc in range(4):
                            sg = qb * 4 + qc
                            j = sg % 2
                            if sg >= 1:
                                v.wait_ge(sem["sLn2d"], sg)
                                v.wait_ge(sem["sYspA"], 16 * sg)
                                v.wait_ge(sem["sYspB"], 16 * sg)
                            for hf in range(2):
                                widx = sg * 2 + hf
                                v.wait_ge(sem["sWoPE"], widx + 1)
                                v.wait_ge(sem["sDv"], dv[0])
                                vraw.tensor_copy(
                                    ybuf[:, hf * 1024:(hf + 1) * 1024],
                                    pool[:, hf * 1024:(hf + 1) * 1024]
                                ).then_inc(sem["sWoEv"], 1)
                            for hf in range(2):
                                widx = sg * 2 + hf
                                v.wait_ge(sem["sXha" if widx % 2 == 0
                                              else "sXhb"],
                                          16 * (widx // 2 + 1))
                                v.wait_ge(sem["sWoEv"], sg * 2 + 2)
                                v.wait_ge(sem["sDv"], dv[0])
                                vraw.tensor_add(
                                    ybuf[:, hf * 1024:(hf + 1) * 1024],
                                    ybuf[:, hf * 1024:(hf + 1) * 1024],
                                    xh[widx % 3][:]
                                ).then_inc(sem["sXhF"], 1)
                                for k in range(2):
                                    v.wait_ge(sem["sXhF"], widx + 1)
                                    v.bn_stats(
                                        stats[:, j * 24 + (hf * 2 + k) * 6:
                                              j * 24 + (hf * 2 + k + 1) * 6],
                                        ybuf[:, hf * 1024 + k * 512:
                                             hf * 1024 + (k + 1) * 512])
                                v.sig(sem["sYd"])
                            v.bn_aggr(aggr[:, j * 2: j * 2 + 2],
                                      stats[:, j * 24: j * 24 + 24])
                            v.sig(sem["sStat"])
                            v.wait_ge(sem["sRstd"], 2 * NT0 + 2 * sg + 2)
                            v.tensor_scalar(nmr[:, j: j + 1],
                                            aggr[:, j * 2: j * 2 + 1],
                                            rstd[:, j: j + 1], -1.0,
                                            OP.mult, OP.mult)
                            v.sig(sem["sNmr"])

        if MAX_PHASE < 3:
            return nc
        pD = ExitStack()
        ctx.enter_context(pD)
        w1q0 = pD.enter_context(nc.sbuf_tensor("w1q0", [128, 64 * 512], BF16))
        bias1_s = pD.enter_context(nc.sbuf_tensor("bias1_s", [1, 1024], BF16))
        pC = ExitStack()
        with pC:
            ln2T = pC.enter_context(nc.sbuf_tensor("ln2T", [128, NI * TOWN], BF16))
            w0b = [pC.enter_context(
                nc.sbuf_tensor(f"w0b{j}", [128, 16 * 1024], BF16))
                for j in range(2)]

            p3a = ExitStack()
            with p3a:
                l2b = [p3a.enter_context(nc.sbuf_tensor(f"l2b{j}", [128, D], BF16))
                       for j in range(2)]
                ptr = [p3a.enter_context(nc.psum_tensor(f"ptr3{j}", [128, 1024], BF16))
                       for j in range(2)]
                bar_n[0] += 1
                with nc.Block() as blk:
                    @blk.sync
                    def _(s):
                        barrier(s)
                        for t in range(NI):
                            if t >= 2:
                                s.wait_ge(sem["sTp"], 2 * NT0 + 2 * (t - 1))
                            s.dma_start(l2b[t % 2][:],
                                        l2_sp[t * 128:(t + 1) * 128, :]
                                        ).then_inc(
                                sem["sL2a" if t % 2 == 0 else "sL2b"], 16)

                    @blk.tensor
                    def _(tn):
                        barrier(tn)
                        for t in range(NI):
                            tn.wait_ge(sem["sL2a" if t % 2 == 0 else "sL2b"],
                                       16 * (t // 2 + 1))
                            for r in range(2):
                                if 2 * t + r >= 2:
                                    tn.wait_ge(sem["sLnT"],
                                               2 * NT0 + 2 * t + r - 1)
                                ins = None
                                for jj in range(8):
                                    ic = 8 * r + jj
                                    ins = tn.transpose(
                                        ptr[r][:, jj * 128:(jj + 1) * 128],
                                        l2b[t % 2][:, ic * 128:(ic + 1) * 128],
                                        idm[:])
                                ins.then_inc(sem["sTp"], 1)

                    @blk.vector
                    def _(vraw):
                        barrier(vraw)
                        v = _VW(vraw)
                        for t in range(NI):
                            for r in range(2):
                                v.wait_ge(sem["sTp"], 2 * NT0 + 2 * t + r + 1)
                                dst = ln2T[:, :].rearrange(
                                    "p (i s) -> p i s", i=NI
                                )[:, 8 * r: 8 * r + 8, t * 128:(t + 1) * 128]
                                src = ptr[r][:, :].rearrange(
                                    "p (i s) -> p i s", i=8)
                                v.tensor_copy(dst, src)
                                v.sig(sem["sLnT"])

                    @blk.scalar
                    def _(scraw):
                        barrier(scraw)
                        sc = _AW(scraw)

                    @blk.gpsimd
                    def _(g):
                        barrier(g)
                        for i in range(NI):
                            g.dma_start(
                                w0b[0][:, i * 1024:(i + 1) * 1024],
                                w0_t[i * 128:(i + 1) * 128, 0:1024]
                            ).then_inc(sem["sWa"], 16)

            p3b = ExitStack()
            with p3b:
                gst = [p3b.enter_context(nc.sbuf_tensor(f"gst{j}", [128, 2048], BF16))
                       for j in range(2)]
                bias0_s = p3b.enter_context(nc.sbuf_tensor("bias0_s", [128, 64], F32))
                pool = p3b.enter_context(nc.psum_tensor("pool3", [128, 4096], F32))
                bar_n[0] += 1
                with nc.Block() as blk:
                    @blk.gpsimd
                    def _(g):
                        barrier(g)
                        for ng in range(1, 8):
                            g.wait_ge(sem["sF1"], 8 * (ng - 1))
                            ws = sem["sWa" if ng % 2 == 0 else "sWb"]
                            for i in range(NI):
                                g.dma_start(
                                    w0b[ng % 2][:, i * 1024:(i + 1) * 1024],
                                    w0_t[i * 128:(i + 1) * 128,
                                         ng * 1024:(ng + 1) * 1024]
                                ).then_inc(ws, 16)
                        g.dma_start(bias1_s[:, 0:512],
                                    bias1_r[:, 0:512]).then_inc(sem["sB1"], 16)
                        for n in range(64):
                            g.dma_start(w1q0[:, n * 512:(n + 1) * 512],
                                        w1_t[n * 128:(n + 1) * 128, 0:512]
                                        ).then_inc(sem["sW4a"], 16)

                    @blk.tensor
                    def _(tn):
                        barrier(tn)
                        tn.wait_ge(sem["sLnT"], 2 * NT0 + 2 * NI)
                        for ng in range(8):
                            tn.wait_ge(sem["sWa" if ng % 2 == 0 else "sWb"],
                                       W1P[ng % 2] + 256 * (ng // 2 + 1))
                            for nl in range(8):
                                nn = ng * 8 + nl
                                slot = nn % 2
                                if nn >= 2:
                                    tn.wait_ge(sem["sGel"], nn - 1)
                                ins = None
                                for i in range(NI):
                                    lhs = w0b[ng % 2][:, i * 1024 + nl * 128:
                                                      i * 1024 + (nl + 1) * 128]
                                    for scv in range(4):
                                        ins = tn.matmul(
                                            pool[:, slot * 2048 + scv * 512:
                                                 slot * 2048 + (scv + 1) * 512],
                                            lhs,
                                            ln2T[:, i * TOWN + scv * 512:
                                                 i * TOWN + (scv + 1) * 512],
                                            start=(i == 0), stop=(i == NI - 1))
                                ins.then_inc(sem["sF1"], 1)

                    @blk.scalar
                    def _(scraw):
                        barrier(scraw)
                        sc = _AW(scraw)
                        sc.wait_ge(sem["sMisc"], 64)
                        for nn in range(64):
                            slot = nn % 2
                            sc.wait_ge(sem["sF1"], nn + 1)
                            if nn >= 2:
                                sc.wait_ge(sem["sGa" if nn % 2 == 0 else "sGb"],
                                           16 * (nn // 2))
                            sc.activation(gst[slot][:],
                                          pool[:, slot * 2048:
                                               slot * 2048 + 2048],
                                          AF.Gelu,
                                          bias=bias0_s[:, nn: nn + 1])
                            sc.sig(sem["sGel"])

                    @blk.sync
                    def _(s):
                        barrier(s)
                        s.dma_start(bias0_s[:], bias0_t).then_inc(sem["sMisc"], 16)
                        for nn in range(64):
                            s.wait_ge(sem["sGel"], nn + 1)
                            s.dma_start(g_sp[:, nn * 2048:(nn + 1) * 2048],
                                        gst[nn % 2][:]).then_inc(
                                sem["sGa" if nn % 2 == 0 else "sGb"], 16)
                        s.wait_ge(sem["sGa"], 16 * 32)
                        s.wait_ge(sem["sGb"], 16 * 32)

                    @blk.vector
                    def _(vraw):
                        barrier(vraw)
                        v = _VW(vraw)

        if MAX_PHASE < 4:
            return nc
        p4 = ExitStack()
        with p4:
            w1q1 = p4.enter_context(
                nc.sbuf_tensor("w1q1", [128, 64 * 512], BF16))
            w1q = [w1q0, w1q1]
            gq = [p4.enter_context(nc.sbuf_tensor(f"gq{j}", [128, 64 * 256], BF16))
                  for j in range(2)]
            yh = [p4.enter_context(nc.sbuf_tensor(f"yh{j}", [128, 512], F32))
                  for j in range(2)]
            obuf = [p4.enter_context(nc.sbuf_tensor(f"obuf{j}", [128, 512], F32))
                    for j in range(2)]
            pool = p4.enter_context(nc.psum_tensor("pool4", [128, 2048], F32))
            bar_n[0] += 1

            with nc.Block() as blk:
                @blk.gpsimd
                def _(g):
                    barrier(g)
                    gv = g_sp.rearrange("p (n s) -> p n s", s=2048)
                    for hq4 in range(4):
                        for sq in range(8):
                            gqi = hq4 * 8 + sq
                            if gqi >= 2:
                                g.wait_ge(sem["sF2"], 2 * (gqi - 1))
                            gsem = sem["sGqa" if gqi % 2 == 0 else "sGqb"]
                            for grp in range(8):
                                gsrc = gv[:, grp * 8:(grp + 1) * 8,
                                          sq * 256:(sq + 1) * 256]
                                dstv = gq[gqi % 2][:, :].rearrange(
                                    "p (n s) -> p n s", s=256)
                                g.dma_start(dstv[:, grp * 8:(grp + 1) * 8, :],
                                            gsrc).then_inc(gsem, 16)
                            if sq >= 1 and hq4 < 3:
                                nq = hq4 + 1
                                ws = sem["sW4a" if nq % 2 == 0 else "sW4b"]
                                if sq == 1:
                                    g.wait_ge(sem["sF2"], 16 * hq4)
                                    g.dma_start(
                                        bias1_s[:, (nq % 2) * 512:
                                                (nq % 2) * 512 + 512],
                                        bias1_r[:, nq * 512:(nq + 1) * 512]
                                    ).then_inc(
                                        sem["sB1" if nq % 2 == 0 else "sB1b"],
                                        16)
                                n0 = (sq - 1) * 10
                                for n in range(n0, min(n0 + 10, 64)):
                                    g.dma_start(
                                        w1q[nq % 2][:, n * 512:(n + 1) * 512],
                                        w1_t[n * 128:(n + 1) * 128,
                                             nq * 512:(nq + 1) * 512]
                                    ).then_inc(ws, 16)

                @blk.sync
                def _(s):
                    barrier(s)
                    fi = 0
                    for hq4 in range(4):
                        for sq in range(8):
                            for s1 in range(2):
                                sg = sq * 2 + s1
                                if fi >= 2:
                                    s.wait_ge(sem["sOb"], fi - 1)
                                s.dma_start(yh[fi % 2][:],
                                            y_sp[sg * 128:(sg + 1) * 128,
                                                 hq4 * 512:(hq4 + 1) * 512]
                                            ).then_inc(
                                    sem["sYha" if fi % 2 == 0 else "sYhb"], 16)
                                s.wait_ge(sem["sOb"], fi + 1)
                                s.dma_start(
                                    out[sg * 128:(sg + 1) * 128,
                                        hq4 * 512:(hq4 + 1) * 512],
                                    obuf[fi % 2][:]
                                ).then_inc(
                                    sem["sOua" if fi % 2 == 0 else "sOub"], 16)
                                fi += 1
                    s.wait_ge(sem["sOua"], 16 * 32)
                    s.wait_ge(sem["sOub"], 16 * 32)

                @blk.tensor
                def _(tn):
                    barrier(tn)
                    fi = 0
                    for hq4 in range(4):
                        tn.wait_ge(sem["sB1" if hq4 % 2 == 0 else "sB1b"],
                                   16 * (hq4 // 2 + 1))
                        tn.wait_ge(sem["sW4a" if hq4 % 2 == 0 else "sW4b"],
                                   1024 * (hq4 // 2 + 1))
                        for sq in range(8):
                            gqi = hq4 * 8 + sq
                            tn.wait_ge(sem["sGqa" if gqi % 2 == 0 else "sGqb"],
                                       16 * 8 * (gqi // 2 + 1))
                            for s1 in range(2):
                                pp = pool[:, (fi % 4) * 512:
                                          (fi % 4) * 512 + 512]
                                if fi >= 4:
                                    tn.wait_ge(sem["sOb"], fi - 3)
                                for n in range(64):
                                    lhs = gq[gqi % 2][:, n * 256 + s1 * 128:
                                                      n * 256 + (s1 + 1) * 128]
                                    tn.matmul(pp, lhs,
                                              w1q[hq4 % 2][:, n * 512:
                                                           (n + 1) * 512],
                                              start=(n == 0), stop=False)
                                tn.matmul(pp, ones1[:],
                                          bias1_s[:, (hq4 % 2) * 512:
                                                  (hq4 % 2) * 512 + 512],
                                          start=False, stop=True
                                          ).then_inc(sem["sF2"], 1)
                                fi += 1

                @blk.vector
                def _(vraw):
                    barrier(vraw)
                    v = _VW(vraw)
                    fi = 0
                    for hq4 in range(4):
                        for sq in range(8):
                            for s1 in range(2):
                                v.wait_ge(sem["sF2"], fi + 1)
                                v.wait_ge(sem["sYha" if fi % 2 == 0 else "sYhb"],
                                          16 * (fi // 2 + 1))
                                if fi >= 2:
                                    v.wait_ge(sem["sOua" if fi % 2 == 0
                                                  else "sOub"],
                                              16 * (fi // 2))
                                v.tensor_add(
                                    obuf[fi % 2][:],
                                    pool[:, (fi % 4) * 512:
                                         (fi % 4) * 512 + 512],
                                    yh[fi % 2][:])
                                v.sig(sem["sOb"])
                                fi += 1

                @blk.scalar
                def _(scraw):
                    barrier(scraw)
                    sc = _AW(scraw)

    return nc


def _host_prep(inputs):
    x = np.asarray(inputs["L0_attn_x"], dtype=np.float32)
    wq = np.asarray(inputs["L0_attn_Wq"], dtype=np.float32)
    wk = np.asarray(inputs["L0_attn_Wk"], dtype=np.float32)
    wv = np.asarray(inputs["L0_attn_Wv"], dtype=np.float32)
    wo = np.asarray(inputs["L0_attn_Wo"], dtype=np.float32)
    w0 = np.asarray(inputs["L0_ffn_W0"], dtype=np.float32)
    b0 = np.asarray(inputs["L0_ffn_bias0"], dtype=np.float32)
    w1 = np.asarray(inputs["L0_ffn_W1"], dtype=np.float32)
    b1 = np.asarray(inputs["L0_ffn_bias1"], dtype=np.float32)

    perm = np.concatenate([
        h * HD + np.concatenate([np.arange(0, HD, 2), np.arange(1, HD, 2)])
        for h in range(H)])
    wq_t = np.ascontiguousarray(wq.T[:, perm])
    wk_t = np.ascontiguousarray(wk.T[:, perm])
    wv_t = np.ascontiguousarray(wv.T)
    wo_t = np.ascontiguousarray(wo.T)
    w0_t = np.ascontiguousarray(w0.T)
    w1_t = np.ascontiguousarray(w1.T)
    bias0_t = np.ascontiguousarray(b0.reshape(64, 128).T)
    bias1_r = np.ascontiguousarray(b1.reshape(1, D))

    idm = np.eye(128, dtype=BF)

    def band_mask():
        p = np.arange(128)[:, None]
        w = np.arange(640)[None, :]
        valid = (w > p) & (w <= p + WIN)
        return np.where(valid, 0.0, NEGM).astype(np.float32)

    std = band_mask()

    def per_core_inputs(core):
        q = core % 4
        b = core // 4
        r0 = q * TOWN - WIN
        if q == 0:
            xc = np.concatenate(
                [np.zeros((WIN, D), np.float32), x[b, 0:TOWN]], axis=0)
        else:
            xc = x[b, r0: r0 + TOWN + WIN]
        xc = np.ascontiguousarray(xc)

        pos = np.maximum(np.arange(r0, r0 + CTX), 0).astype(np.float32)
        inv = (1.0 / (10000.0 ** (np.arange(0, HD, 2, dtype=np.float32) / HD))
               ).astype(np.float32)
        fr = pos[None, :] * inv[:, None]
        c64 = np.cos(fr).astype(np.float32)
        s64 = np.sin(fr).astype(np.float32)
        cc = np.concatenate([c64, c64], axis=0)
        ss = np.concatenate([-s64, s64], axis=0)

        slabs = [std]
        for qc in range(4):
            m = std.copy()
            if q == 0:
                kpos = qc * 128 + np.arange(640)[None, :]
                m = np.where(kpos >= WIN, m, NEGM).astype(np.float32)
            slabs.append(m)
        msk = np.concatenate(slabs, axis=1).astype(BF)

        return dict(x_ctx=xc, wq_t=wq_t, wk_t=wk_t, wv_t=wv_t, wo_t=wo_t,
                    w0_t=w0_t, w1_t=w1_t, bias0_t=bias0_t, bias1_r=bias1_r,
                    cc=np.ascontiguousarray(cc), ss=np.ascontiguousarray(ss),
                    msk=np.ascontiguousarray(msk), idm=idm)

    return [per_core_inputs(c) for c in range(8)]


def kernel(**inputs):
    in_maps = _host_prep(inputs)
    nc = build_nc()
    res = run_bass_kernel_spmd(nc, in_maps, core_ids=list(range(8)))
    x = np.asarray(inputs["L0_attn_x"])
    full = np.empty((B, S, D), dtype=np.float32)
    for c in range(8):
        b, q = c // 4, c % 4
        full[b, q * TOWN:(q + 1) * TOWN, :] = res.results[c]["out"]
    return full
